# version 15
# speedup vs baseline: 1.1384x; 1.0065x over previous
import os
import numpy as np
from contextlib import ExitStack

KTRUNC = int(os.environ.get("KTRUNC", "10"))
KP6 = int(os.environ.get("KP6", "3"))
KDBG_LVL = 0
KDBG = False
KDUMP = os.environ.get("KDUMP", "")
KWSEL = int(os.environ.get("KWSEL", "0"))
KHSEL = int(os.environ.get("KHSEL", "0"))

import concourse.bass as bass
import concourse.mybir as mybir
from concourse.bass_utils import run_bass_kernel_spmd

F32 = mybir.dt.float32
F16 = mybir.dt.float16
BF16 = mybir.dt.bfloat16
I16 = mybir.dt.int16
AF = mybir.ActivationFunctionType
ALU = mybir.AluOpType
AX = mybir.AxisListType

SEQ, D, NC = 16384, 1024, 8
TOK = SEQ // NC
HALO = 128
TOKH = TOK + 2 * HALO
H, DH, WS = 8, 64, 128
W = TOK // WS
HID = 3072
NCH, NHC = 8, 24
QMAX = 32767.0
MAGIC = 12582912.0
NEG = -3.4028234663852886e+38
LN_EPS = 1e-5
LN2 = float(np.log(2.0))
U16C = float(np.float32(1.0) - np.float32(2.0 ** -16))

TT_N = [512, 512, 512, 512, 256]
TT_O = [0, 512, 1024, 1536, 2048]
G4 = [0, 512, 1024, 1536]

_CACHE = {}


def _tables():
    d = np.arange(DH)
    d2 = (d % 32).astype(np.float64)
    invf = 10000.0 ** (-(2.0 * d2) / DH)
    sbase = (2.0 * d2 + 0.4 * DH) / (1.4 * DH)
    sgn = np.where(d < 32, -1.0, 1.0)
    t = {}
    for o in (0, 128, 256):
        i = np.arange(WS)[None, :]
        ph = (i + o) * invf[:, None]
        sc = sbase[:, None] ** ((i + o - 192.0) / 64.0)
        t[f"aq{o}"] = np.tile(np.cos(ph) * sc * DH ** -0.5, (2, 1)).astype(np.float32)
        t[f"bq{o}"] = np.tile(sgn[:, None] * np.sin(ph) * sc * DH ** -0.5,
                              (2, 1)).astype(np.float32)
    j = np.arange(WS)[None, :]
    ph = j * invf[:, None]
    sc = sbase[:, None] ** ((192.0 - j) / 64.0)
    t["ak"] = np.tile(np.cos(ph) * sc, (2, 1)).astype(np.float32)
    t["bk"] = np.tile(sgn[:, None] * np.sin(ph) * sc, (2, 1)).astype(np.float32)

    i = np.arange(WS)[:, None]
    jj = np.arange(WS)[None, :]
    left = np.where(i > jj, NEG, 0.0)
    right = np.where(i < jj, NEG, 0.0)
    zero = np.zeros((WS, WS))
    allneg = np.full((WS, WS), NEG)
    t["mneg_mid"] = np.concatenate([left, zero, right], 1).astype(np.float32)
    t["mneg_w0"] = np.concatenate([allneg, zero, right], 1).astype(np.float32)
    t["mneg_w15"] = np.concatenate([left, zero, allneg], 1).astype(np.float32)
    k = np.arange(WS)[:, None]
    q = np.arange(WS)[None, :]
    t["m01_l"] = (q <= k).astype(np.float32)
    t["m01_r"] = (q >= k).astype(np.float32)
    t["m01_z"] = np.zeros((WS, WS), np.float32)
    t["m01_lw0"] = t["m01_l"]
    t["m01_rw15"] = t["m01_r"]
    t["ident"] = np.eye(128, dtype=np.float32)
    t["ones_row"] = np.ones((1, 128), np.float32)
    t["ones_col"] = np.ones((128, 1), np.float32)
    t["ones_tok"] = np.ones((1, 512), np.float32)
    return t


TABLES = _tables()
TBF = ["aq0", "bq0", "aq128", "bq128", "aq256", "bq256", "ak", "bk",
       "m01_l", "m01_r", "m01_lw0", "m01_rw15", "mneg_mid", "mneg_w0",
       "mneg_w15", "ones_row", "ones_col", "ones_tok"]


def build_program(counts=None):
    record = counts is None
    counts = {} if counts is None else counts
    cum = {}
    nc = bass.Bass("TRN2", target_bir_lowering=False, debug=False)

    x_in = nc.dram_tensor("x_pad", [D, TOKH], F32, kind="ExternalInput").ap()
    lng_in = nc.dram_tensor("lng_c", [128, 8], F32, kind="ExternalInput").ap()
    lnb_in = nc.dram_tensor("lnb_c", [128, 8], F32, kind="ExternalInput").ap()
    c_in = nc.dram_tensor("c_cols", [128, 8], F32, kind="ExternalInput").ap()
    wmod_in = nc.dram_tensor("wmod_sl", [D, 768], F32, kind="ExternalInput").ap()
    wqkv_in = nc.dram_tensor("wqkv_sl", [128, D], F32, kind="ExternalInput").ap()
    wout_in = nc.dram_tensor("wout_sl", [128, 512], F32, kind="ExternalInput").ap()
    wg_in = nc.dram_tensor("wg_sl", [384, D], F32, kind="ExternalInput").ap()
    wu_in = nc.dram_tensor("wu_sl", [384, D], F32, kind="ExternalInput").ap()
    wd_in = nc.dram_tensor("wd_sl", [128, HID], F32, kind="ExternalInput").ap()
    sel_in = nc.dram_tensor("sel", [128, 8], F32, kind="ExternalInput").ap()
    tab_in = {n: nc.dram_tensor("t_" + n, list(TABLES[n].shape), F32,
                                kind="ExternalInput").ap() for n in TBF + ["ident"]}
    out_h = nc.dram_tensor("out", [D, TOK], F32, kind="ExternalOutput").ap()

    dbg_kind = {"kind": "ExternalOutput"} if KDBG else {}
    ag0_in = nc.dram_tensor("ag0_in", [1, 768], F32)
    ag0_out = nc.dram_tensor("ag0_out", [8, 768], F32)
    dbg_ag = nc.dram_tensor("dbg_ag", [40, 8], F32)
    ag1_in = nc.dram_tensor("ag1_in", [12 * 128, 128], BF16)
    ag1_out = nc.dram_tensor("ag1_out", [96 * 128, 128], BF16)
    ag2_in = nc.dram_tensor("ag2_in", [72 * 128, 128], BF16)
    ag2_out = nc.dram_tensor("ag2_out", [576 * 128, 128], BF16)
    agO_in = [nc.dram_tensor(f"agO_in{i}", [1, 8], F32) for i in range(4)]
    agO_out = [nc.dram_tensor(f"agO_out{i}", [8, 8], F32) for i in range(4)]
    rows_bnc = nc.dram_tensor("rows_bnc", [48, 128], BF16)
    nbias_bnc = nc.dram_tensor("nbias_bnc", [128, 128], BF16)
    x2_h = nc.dram_tensor("x2_h", [D, TOK], F32)

    h2_h = nc.dram_tensor("h2_h", [D, TOK], F16)
    gate_h = nc.dram_tensor("gate_h", [HID, TOK], F16)
    v_h = nc.dram_tensor("v_h", [18 * 128, 512], BF16)
    park = nc.dram_tensor("park", [128, 40960], mybir.dt.uint16)
    g2_h = nc.dram_tensor("g2_h", [HID, TOK], F16)

    ag1o, ag2o = ag1_out.ap(), ag2_out.ap()

    def qkv_tidx(ic, oc):
        return oc * 12 + ic

    def wo_tidx(ic, oc):
        return oc * 12 + 8 + ic

    def wgud_tidx(which, ic, oc):
        if which < 2:
            return (oc // 3) * 72 + which * 24 + (oc % 3) * 8 + ic
        return oc * 72 + 48 + ic

    es = ExitStack()
    with es:
        def sbuf(name, shape, dt):
            return es.enter_context(nc.sbuf_tensor(name, shape, dt))

        def psumt(name, shape, dt=F32):
            return es.enter_context(nc.psum_tensor(name, shape, dt))

        AR_BIG = sbuf("AR_BIG", [128, 49152], F16)
        AR_MID = sbuf("AR_MID", [128, 16384], BF16)
        AR_SM = sbuf("AR_SM", [128, 8192], BF16)
        AR_FX = sbuf("AR_FX", [128, 3072], F32)
        AR_ROW = sbuf("AR_ROW", [128, 3584], F32)
        vrows = sbuf("vrows", [1, 6144], BF16)
        tb = {n: sbuf("c_" + n, list(TABLES[n].shape), BF16) for n in TBF}
        tb_or32 = sbuf("c_or32", [1, 128], F32)
        tident = sbuf("c_ident", [128, 128], F32)
        sel_s = sbuf("sel_s", [128, 8], F32)
        modv = sbuf("modv", [128, 48], F32)
        vec_c = sbuf("vec_c", [128, 64], F32)
        silu_s = sbuf("silu_s", [128, 8], F32)
        scal = sbuf("scal", [1, 96], F32)
        scol = sbuf("scol", [128, 16], F32)
        Mall = sbuf("Mall", [128, 128], F32)
        Sall = sbuf("Sall", [128, 128], F32)
        biasM = sbuf("biasM", [128, 128], F32)
        amaxA = sbuf("amaxA", [128, 2048], BF16)
        wtile = sbuf("wtile", [128, 2048], BF16)
        wvb = sbuf("wvb", [128, 1024], BF16)
        wblk = sbuf("wblk", [128, 96], F32)
        wblk2 = sbuf("wblk2", [128, 96], F32)
        rowslot = sbuf("rowslot", [1, 2048], BF16)
        cmag = sbuf("cmag", [128, 6], F32)

        def vf32(ar, u16_off, n):
            assert u16_off % 2 == 0
            return ar[:].bitcast(F32)[:, u16_off // 2: u16_off // 2 + n]

        def vbf(ar, u16_off, n):
            return ar[:].bitcast(BF16)[:, u16_off: u16_off + n]

        def vf16(ar, u16_off, n):
            return ar[:].bitcast(F16)[:, u16_off: u16_off + n]

        def vi16(ar, u16_off, n):
            return ar[:].bitcast(I16)[:, u16_off: u16_off + n]

        PS_A = psumt("PS_A", [128, 2048])
        PS_B = psumt("PS_B", [128, 1536])
        PS_C = psumt("PS_C", [128, 512])

        s_ld = es.enter_context(nc.semaphore("s_ld"))
        s_sp = es.enter_context(nc.semaphore("s_sp"))
        s_cc = es.enter_context(nc.semaphore("s_cc"))
        s_v = es.enter_context(nc.semaphore("s_v"))
        s_s = es.enter_context(nc.semaphore("s_s"))
        s_t = es.enter_context(nc.semaphore("s_t"))
        s_g = es.enter_context(nc.semaphore("s_g"))

        block = es.enter_context(nc.Block())

        def engine_ctx(eng):
            def I(inst, sm, n=1):
                cum[sm.name] = cum.get(sm.name, 0) + n
                inst.then_inc(sm, n)
                return inst

            def M(tag, sm):
                key = (tag, sm.name)
                v = cum.get(sm.name, 0)
                if record:
                    assert key not in counts, f"duplicate mark {key}"
                    counts[key] = v
                else:
                    assert counts.get(key) == v, \
                        f"mark drift {key}: pass1={counts.get(key)} pass2={v}"

            def WT(tag, sm):
                if not record:
                    eng.wait_ge(sm, counts[(tag, sm.name)])

            def WSelf(sm):
                if not record:
                    eng.wait_ge(sm, cum.get(sm.name, 0))

            def IW(inst, sm, n=1):
                I(inst, sm, n)
                WSelf(sm)
                return inst
            return I, M, WT, WSelf, IW

        def emit(E, eng):
            I, M, WT, WSelf, IW = engine_ctx(eng)
            tld = AR_FX[:, 0:512]

            if E == "sync":
                I(eng.dma_start(out=tident[:], in_=tab_in["ident"]), s_ld, 16)
                I(eng.dma_start(out=sel_s[:], in_=sel_in), s_ld, 16)
                I(eng.dma_start(out=vec_c[:, 16:24], in_=lng_in), s_ld, 16)
                I(eng.dma_start(out=vec_c[:, 24:32], in_=lnb_in), s_ld, 16)
                I(eng.dma_start(out=silu_s[:], in_=c_in), s_ld, 16)
                M("p0a", s_ld)
                for i, n in enumerate(TBF):
                    if i >= 1:
                        WT(f"p0tb{i - 1}", s_s)
                    sh = TABLES[n].shape
                    I(eng.dma_start(out=tld[0:sh[0], 0:sh[1]], in_=tab_in[n]), s_ld, 16)
                    M(f"p0t{i}", s_ld)
            if E == "scalar":
                for i, n in enumerate(TBF):
                    sh = TABLES[n].shape
                    WT(f"p0t{i}", s_ld)
                    I(eng.copy(tb[n][:], tld[0:sh[0], 0:sh[1]]), s_s)
                    M(f"p0tb{i}", s_s)
                M("p0tb_all", s_s)
                WT("p0a", s_ld)
                I(eng.activation(silu_s[:], silu_s[:], AF.Silu, bias=0.0, scale=1.0),
                  s_s)
                M("p0silu", s_s)
            if E == "vector":
                I(eng.memset(cmag[:, 0:1], MAGIC), s_v)
                I(eng.memset(cmag[:, 1:2], -MAGIC), s_v)
                I(eng.memset(cmag[:, 2:3], 0.5), s_v)
                I(eng.memset(cmag[:, 3:4], -0.25), s_v)
                I(eng.memset(cmag[:, 4:5], 0.25), s_v)
                M("p0cmag", s_v)
                WT("p0tb_all", s_s)
                I(eng.tensor_tensor(tb_or32[:], tb["ones_row"][:], tb["ones_row"][:], ALU.max), s_v)
                M("p0or32", s_v)

            wslm = vf32(AR_BIG, 0, 6144)
            if E == "sync":
                for ch in range(8):
                    I(eng.dma_start(out=wslm[:, ch * 768:(ch + 1) * 768],
                                    in_=wmod_in[ch * 128:(ch + 1) * 128, :]), s_ld, 16)
                M("p1w", s_ld)
            if E == "tensor":
                WT("p0silu", s_s)
                WT("p1w", s_ld)
                for off, nn in ((0, 512), (512, 256)):
                    for ch in range(8):
                        I(eng.matmul(PS_B[0:1, off:off + nn],
                                     silu_s[:, ch:ch + 1],
                                     wslm[:, ch * 768 + off: ch * 768 + off + nn],
                                     start=(ch == 0), stop=(ch == 7)), s_t)
                M("p1mm", s_t)
            if E == "vector":
                WT("p1mm", s_t)
                I(eng.tensor_scalar(AR_ROW[0:1, 0:768], PS_B[0:1, 0:768], 0.0, None, ALU.add), s_v)
                M("p1ev", s_v)
            if E == "gpsimd":
                WT("p1ev", s_v)
                I(eng.dma_start(out=ag0_in.ap(), in_=AR_ROW[0:1, 0:768]), s_sp, 16)
                WSelf(s_sp)
                I(eng.collective_compute("AllGather", ALU.bypass,
                                         replica_groups=[list(range(8))],
                                         ins=[ag0_in.ap()], outs=[ag0_out.ap()]),
                  s_cc)
                M("ag0", s_cc)
                WSelf(s_cc)
                I(eng.dma_start(out=modv[:],
                                in_=ag0_out.ap().rearrange("r f -> (r f)")
                                .rearrange("(v c p) -> (v c) p", v=6, p=128)
                                .transpose([1, 0]),
                                allow_slow_non_contiguous=True), s_sp, 16)
                M("p1modv", s_sp)
            if E == "scalar":
                WT("p1modv", s_sp)
                WT("p0a", s_ld)
                WT("p0cmag", s_v)
                for src_c, mcol, scol_ in ((modv[:, 16:24], 0, 8),
                                           (modv[:, 40:48], 16, 24)):
                    I(eng.activation(biasM[:, mcol:mcol + 8], src_c, AF.Abs,
                                     bias=0.0, scale=1.0), s_s)
                    I(eng.activation(biasM[:, mcol:mcol + 8], biasM[:, mcol:mcol + 8],
                                     AF.Relu, bias=cmag[:, 3:4], scale=1.0), s_s)
                    I(eng.activation(biasM[:, mcol:mcol + 8], biasM[:, mcol:mcol + 8],
                                     AF.Identity, bias=cmag[:, 4:5], scale=1.0), s_s)
                    I(eng.sign(biasM[:, scol_:scol_ + 8], src_c), s_s)
                    I(eng.sign(biasM[:, scol_:scol_ + 8], biasM[:, scol_:scol_ + 8],
                               bias=cmag[:, 2:3]), s_s)
                M("p1ga_s", s_s)
            if E == "vector":
                WT("p1modv", s_sp)
                WT("p0a", s_ld)
                sa_b, ca_b = modv[:, 0:8], modv[:, 8:16]
                sbb_b, cb_b = modv[:, 24:32], modv[:, 32:40]
                IW(eng.tensor_scalar(vec_c[:, 0:8], ca_b, 1.0, None, ALU.add), s_v)
                IW(eng.tensor_scalar(vec_c[:, 8:16], sa_b, 0.0, None, ALU.add), s_v)
                IW(eng.tensor_scalar(vec_c[:, 32:40], cb_b, 1.0, None, ALU.add), s_v)
                IW(eng.tensor_scalar(vec_c[:, 40:48], sbb_b, 0.0, None, ALU.add), s_v)
                WT("p1ga_s", s_s)
                IW(eng.tensor_tensor(vec_c[:, 48:56], biasM[:, 0:8], biasM[:, 8:16],
                                    ALU.mult), s_v)
                IW(eng.tensor_tensor(vec_c[:, 56:64], biasM[:, 16:24], biasM[:, 24:32],
                                    ALU.mult), s_v)
                for srcc, outc in ((vec_c[:, 48:56], 0), (vec_c[:, 56:64], 1)):
                    IW(eng.tensor_tensor(scol[:, 8:16], srcc, sel_s[:], ALU.mult), s_v)
                    IW(eng.tensor_tensor(scol[:, 8:12], scol[:, 8:12], scol[:, 12:16],
                                        ALU.add), s_v)
                    IW(eng.tensor_tensor(scol[:, 8:10], scol[:, 8:10], scol[:, 10:12],
                                        ALU.add), s_v)
                    IW(eng.tensor_tensor(scol[:, outc:outc + 1], scol[:, 8:9],
                                        scol[:, 9:10], ALU.add), s_v)
                M("p1cols", s_v)
            if E == "tensor":
                WT("p1cols", s_v)
                WT("p0a", s_ld)
                I(eng.transpose(PS_C[0:48, 0:128], vec_c[:, 0:48], tident[:]), s_t)
                M("p1tr", s_t)
            if E == "scalar":
                WT("p1tr", s_t)
                I(eng.copy(vbf(AR_MID, 0, 128)[0:48, :], PS_C[0:48, 0:128]), s_s)
                M("p1trev", s_s)
            if E == "gpsimd":
                WT("p1trev", s_s)
                I(eng.dma_start(out=rows_bnc.ap(), in_=vbf(AR_MID, 0, 128)[0:48, :]),
                  s_sp, 16)
                WSelf(s_sp)
                I(eng.dma_start(out=vrows[:],
                                in_=rows_bnc.ap().rearrange("a b -> (a b)")
                                .unsqueeze(0)), s_sp, 16)
                M("p1rows", s_sp)

            def row_of(c, v):
                return vrows[0:1, (v * 8 + c) * 128:(v * 8 + c + 1) * 128]

            wsl0 = vf32(AR_BIG, 0, 3072)
            if E == "sync":
                WT("p1mm", s_t)
                I(eng.dma_start(out=wsl0[:, 0:1024], in_=wqkv_in), s_ld, 16)
                I(eng.dma_start(out=wsl0[:, 1024:1536], in_=wout_in), s_ld, 16)
                M("p2a_ld", s_ld)
            if E == "vector":
                WT("p2a_ld", s_ld)
                I(eng.tensor_scalar(wsl0[:, 1024:1536], wsl0[:, 1024:1536],
                                    scol[:, 0:1], None, ALU.mult), s_v)
                M("p2a_ga", s_v)
            if E == "tensor":
                WT("p2a_ld", s_ld)
                WT("p2a_ga", s_v)
                for grp in range(3):
                    if grp > 0:
                        WT(f"p2a_ev{grp - 1}", s_s)
                    for j in range(4):
                        it = grp * 4 + j
                        I(eng.transpose(PS_A[:, j * 512:j * 512 + 128],
                                        wsl0[:, it * 128:(it + 1) * 128],
                                        tident[:]), s_t)
                    M(f"p2a_tr{grp}", s_t)
            if E == "scalar":
                for grp in range(3):
                    WT(f"p2a_tr{grp}", s_t)
                    for j in range(4):
                        it = grp * 4 + j
                        I(eng.copy(vbf(AR_BIG, 16384 + it * 128, 128)[:, 0:128],
                                   PS_A[:, j * 512:j * 512 + 128]), s_s)
                    M(f"p2a_ev{grp}", s_s)
            if E == "gpsimd":
                if KDUMP:
                    WT("p2a_ga", s_v)
                    I(eng.dma_start(out=park.ap()[:, 35584:35616],
                                    in_=scol[:].bitcast(mybir.dt.uint16)), s_sp, 16)
                    I(eng.dma_start(out=park.ap()[:, 35616:35744],
                                    in_=vec_c[:].bitcast(mybir.dt.uint16)), s_sp, 16)
                    M("park_scol", s_sp)
                for grp in range(3):
                    WT(f"p2a_ev{grp}", s_s)
                for it in range(12):
                    I(eng.dma_start(out=ag1_in.ap()[it * 128:(it + 1) * 128, :],
                                    in_=vbf(AR_BIG, 16384 + it * 128, 128)[:, 0:128]),
                      s_sp, 16)
                WSelf(s_sp)
                I(eng.collective_compute("AllGather", ALU.bypass,
                                         replica_groups=[list(range(8))],
                                         ins=[ag1_in.ap()], outs=[ag1_out.ap()]),
                  s_cc)
                M("ag1", s_cc)

            grp_src = [(wg_in, 0, 1024, 0), (wg_in, 128, 1024, 0), (wg_in, 256, 1024, 0),
                       (wu_in, 0, 1024, 1), (wu_in, 128, 1024, 1), (wu_in, 256, 1024, 1),
                       (wd_in, 0, 3072, 2)]
            wdq = vf32(AR_BIG, 8192, 3072)
            for gi, (src, roff, ninc, which) in enumerate(grp_src):
                nblk = ninc // 32
                if E == "sync":
                    if gi == 0:
                        WT("p2a_tr2", s_t)
                    else:
                        WT(f"p2g{gi - 1}_dq", s_v)
                    I(eng.dma_start(out=wsl0[:, 0:ninc],
                                    in_=src[roff:roff + 128, :]), s_ld, 16)
                    M(f"p2g{gi}_ld", s_ld)
                if E == "vector":
                    WT(f"p2g{gi}_ld", s_ld)
                    if gi > 0:
                        WT(f"p2g{gi - 1}_tr1", s_t)
                    if gi == 0:
                        WT("p1cols", s_v)
                    IW(eng.tensor_reduce(wblk[:, 0:nblk].unsqueeze(2),
                                         wsl0[:, 0:ninc].rearrange("p (b k) -> p b k", k=32),
                                         AX.X, ALU.max, apply_absolute_value=True), s_v)
                    IW(eng.tensor_scalar(wblk[:, 0:nblk], wblk[:, 0:nblk],
                                         1.0 / 7.0, 1e-12, ALU.mult, ALU.max), s_v)
                    IW(eng.reciprocal(wblk2[:, 0:nblk], wblk[:, 0:nblk]), s_v)
                    rrep = wblk2[:, 0:nblk].unsqueeze(2).broadcast_to([128, nblk, 32])
                    srep = wblk[:, 0:nblk].unsqueeze(2).broadcast_to([128, nblk, 32])
                    I(eng.tensor_tensor(wdq[:, 0:ninc].rearrange("p (b k) -> p b k", k=32),
                                        wsl0[:, 0:ninc].rearrange("p (b k) -> p b k", k=32),
                                        rrep, ALU.mult), s_v)
                    I(eng.tensor_scalar(wdq[:, 0:ninc], wdq[:, 0:ninc],
                                        MAGIC, MAGIC, ALU.add, ALU.subtract), s_v)
                    if which == 2:
                        I(eng.scalar_tensor_tensor(
                            wdq[:, 0:ninc].rearrange("p (b k) -> p b k", k=32),
                            wdq[:, 0:ninc].rearrange("p (b k) -> p b k", k=32),
                            scol[:, 1:2], srep, ALU.mult, ALU.mult), s_v)
                    else:
                        I(eng.tensor_tensor(wdq[:, 0:ninc].rearrange("p (b k) -> p b k", k=32),
                                            wdq[:, 0:ninc].rearrange("p (b k) -> p b k", k=32),
                                            srep, ALU.mult), s_v)
                    M(f"p2g{gi}_dq", s_v)
                ntl = ninc // 128
                ngrp = (ntl + 3) // 4
                if E == "tensor":
                    WT(f"p2g{gi}_dq", s_v)
                    for grp in range(ngrp):
                        if grp > 0:
                            WT(f"p2g{gi}_tev{grp - 1}", s_s)
                        elif gi > 0:
                            WT(f"p2g{gi - 1}_tev_last", s_s)
                        else:
                            WT("p2a_ev2", s_s)
                        for j in range(min(4, ntl - grp * 4)):
                            it = grp * 4 + j
                            I(eng.transpose(PS_A[:, j * 512:j * 512 + 128],
                                            wdq[:, it * 128:(it + 1) * 128],
                                            tident[:]), s_t)
                        M(f"p2g{gi}_tr{grp}", s_t)
                if E == "scalar":
                    for grp in range(ngrp):
                        WT(f"p2g{gi}_tr{grp}", s_t)
                        if grp == 0 and gi > 0:
                            WT(f"p2g{gi - 1}_st", s_sp)
                        for j in range(min(4, ntl - grp * 4)):
                            it = grp * 4 + j
                            I(eng.copy(vbf(AR_BIG, 24576 + it * 128, 128)[:, 0:128],
                                       PS_A[:, j * 512:j * 512 + 128]), s_s)
                        M(f"p2g{gi}_tev{grp}", s_s)
                    M(f"p2g{gi}_tev_last", s_s)
                    M(f"p2g{gi}_tev", s_s)
                if E == "gpsimd":
                    WT(f"p2g{gi}_tev", s_s)
                    if KDUMP == "wq6" and gi == 6:
                        I(eng.dma_start(out=park.ap()[:, 37120:37376],
                                        in_=AR_BIG[:].bitcast(mybir.dt.uint16)
                                        [:, 8192:8448]), s_sp, 16)
                        I(eng.dma_start(out=park.ap()[:, 37376:37504],
                                        in_=AR_BIG[:].bitcast(mybir.dt.uint16)
                                        [:, 24576:24704]), s_sp, 16)
                        I(eng.dma_start(out=park.ap()[:, 37504:37536],
                                        in_=scol[:].bitcast(mybir.dt.uint16)
                                        [:, 0:32]), s_sp, 16)
                        I(eng.dma_start(out=park.ap()[:, 37536:37664],
                                        in_=AR_BIG[:].bitcast(mybir.dt.uint16)
                                        [:, 0:128]), s_sp, 16)
                        M("park_wq6", s_sp)
                    if KDUMP == "wq" and gi == 0:
                        I(eng.dma_start(out=park.ap()[:, 35584:35840],
                                        in_=AR_BIG[:].bitcast(mybir.dt.uint16)
                                        [:, 8192:8448]), s_sp, 16)
                        I(eng.dma_start(out=park.ap()[:, 35840:35968],
                                        in_=vbf(AR_BIG, 24576, 128)[:, 0:128]
                                        .bitcast(mybir.dt.uint16) if False else
                                        AR_BIG[:].bitcast(mybir.dt.uint16)
                                        [:, 24576:24704]), s_sp, 16)
                        M("park_wq", s_sp)
                    for it in range(ntl):
                        if which < 2:
                            ti = which * 24 + (gi % 3) * 8 + it
                        else:
                            ti = 48 + it
                        I(eng.dma_start(out=ag2_in.ap()[ti * 128:(ti + 1) * 128, :],
                                        in_=vbf(AR_BIG, 24576 + it * 128, 128)[:, 0:128]),
                          s_sp, 16)
                    if KDUMP == "wq3" and gi == 0:
                        WSelf(s_sp)
                        I(eng.dma_start(out=park.ap()[:, 35584:35712],
                                        in_=ag2_in.ap()[0:128, :]
                                        .bitcast(mybir.dt.uint16)), s_sp, 16)
                        M("park_agin", s_sp)
                    M(f"p2g{gi}_st", s_sp)
            if E == "gpsimd":
                WSelf(s_sp)
                if KDUMP == "wq4":
                    I(eng.dma_start(out=park.ap()[:, 35968:36096],
                                    in_=ag2_in.ap()[0:128, :]
                                    .bitcast(mybir.dt.uint16)), s_sp, 16)
                    I(eng.dma_start(out=park.ap()[:, 36096:36224],
                                    in_=ag2_in.ap()[48 * 128:49 * 128, :]
                                    .bitcast(mybir.dt.uint16)), s_sp, 16)
                    WSelf(s_sp)
                I(eng.collective_compute("AllGather", ALU.bypass,
                                         replica_groups=[list(range(8))],
                                         ins=[ag2_in.ap()], outs=[ag2_out.ap()]),
                  s_cc)
                M("ag2", s_cc)
                if KDUMP == "wq4":
                    WSelf(s_cc)
                    I(eng.dma_start(out=park.ap()[:, 36224:36352],
                                    in_=ag2_out.ap()[0:128, :]
                                    .bitcast(mybir.dt.uint16)), s_sp, 16)
                    I(eng.dma_start(out=park.ap()[:, 36352:36480],
                                    in_=ag2_out.ap()[72 * 128:73 * 128, :]
                                    .bitcast(mybir.dt.uint16)), s_sp, 16)
                    WSelf(s_sp)

            if KTRUNC >= 3:
                emit_p3(E, eng, I, M, WT, WSelf, IW, row_of)
            if KTRUNC >= 4:
                emit_p4(E, eng, I, M, WT, WSelf, IW)
            if KTRUNC >= 5:
                emit_p5(E, eng, I, M, WT, WSelf, IW)
            if KTRUNC >= 6:
                emit_p6(E, eng, I, M, WT, WSelf, IW, row_of)
            if KTRUNC >= 7:
                emit_p7(E, eng, I, M, WT, WSelf, IW)
            if KTRUNC >= 8:
                emit_p8(E, eng, I, M, WT, WSelf, IW)
            if KTRUNC >= 9:
                emit_p9(E, eng, I, M, WT, WSelf, IW)
            if KTRUNC >= 10:
                emit_p10(E, eng, I, M, WT, WSelf, IW)

        def emit_p3(E, eng, I, M, WT, WSelf, IW, row_of):
            xt = vf32(AR_BIG, 0, 4096)
            bft = vbf(AR_SM, 0, 4096)
            xnt = vbf(AR_MID, 0, 4096)
            qkb = vbf(AR_MID, 4096, 9216)
            vstg = vbf(AR_SM, 4096, 512)
            MU = AR_ROW[0:1, 0:512]
            VA = AR_ROW[0:1, 512:1024]
            INV = AR_ROW[0:1, 1024:1536]
            nm_bf = AR_ROW[:].bitcast(BF16)[0:1, 3072:3584]
            inv_bf = AR_ROW[:].bitcast(BF16)[0:1, 3584:4096]

            def ln_stage(ti, n, stage, src_f32, gcol_base, rows_off, out_ap,
                         out_note):
                tag = f"p3t{ti}s{stage}"
                if E == "sync":
                    if ti > 0 or stage > 0:
                        ptag = (f"p3t{ti}s0" if stage else f"p3t{ti - 1}s1")
                        WT(ptag + "_b27", s_t)
                    I(eng.dma_start(out=rowslot[0:1, 0:2048],
                                    in_=rows_bnc.ap().rearrange("a b -> (a b)")
                                    .unsqueeze(0)[0:1, rows_off:rows_off + 2048]),
                      s_ld, 16)
                    M(tag + "_rows", s_ld)
                if E == "scalar":
                    if stage == 0:
                        WT(f"p3t{ti}_x", s_ld)
                        if ti > 0:
                            WT(f"p3t{ti - 1}s1_mm2", s_t)
                    else:
                        WT(f"p3t{ti}s0_out", s_v)
                    I(eng.copy(bft[:, 0:4096], src_f32[:, 0:4096]), s_s)
                    M(tag + "_c1", s_s)
                    WT(tag + "_mm1", s_t)
                    I(eng.activation(bft[:, 0:4096], src_f32[:, 0:4096],
                                     AF.Square, bias=0.0, scale=1.0), s_s)
                    M(tag + "_c2", s_s)
                if E == "tensor":
                    WT(tag + "_c1", s_s)
                    if ti > 0 or stage > 0:
                        WT(f"p3row_prev_{ti}_{stage}", s_v)
                    for c in range(8):
                        I(eng.matmul(PS_B[0:1, 0:n], tb["ones_col"][:],
                                     bft[:, c * 512:c * 512 + n],
                                     start=(c == 0), stop=(c == 7)), s_t)
                    M(tag + "_mm1", s_t)
                    WT(tag + "_c2", s_s)
                    for c in range(8):
                        I(eng.matmul(PS_B[0:1, 512:512 + n], tb["ones_col"][:],
                                     bft[:, c * 512:c * 512 + n],
                                     start=(c == 0), stop=(c == 7)), s_t)
                    M(tag + "_mm2", s_t)
                if E == "vector":
                    WT(tag + "_mm2", s_t)
                    IW(eng.tensor_scalar(MU[0:1, 0:n], PS_B[0:1, 0:n],
                                        1.0 / 1024.0, None, ALU.mult), s_v)
                    IW(eng.tensor_scalar(VA[0:1, 0:n], PS_B[0:1, 512:512 + n],
                                        1.0 / 1024.0, None, ALU.mult), s_v)
                    IW(eng.tensor_tensor(INV[0:1, 0:n], MU[0:1, 0:n], MU[0:1, 0:n],
                                        ALU.mult), s_v)
                    IW(eng.tensor_tensor(VA[0:1, 0:n], VA[0:1, 0:n], INV[0:1, 0:n],
                                        ALU.subtract), s_v)
                    IW(eng.tensor_scalar(VA[0:1, 0:n], VA[0:1, 0:n], LN_EPS, None,
                                        ALU.add), s_v)
                    IW(eng.reciprocal(VA[0:1, 0:n], VA[0:1, 0:n]), s_v)
                    M(tag + "_v1", s_v)
                if E == "scalar":
                    WT(tag + "_v1", s_v)
                    I(eng.activation(INV[0:1, 0:n], VA[0:1, 0:n], AF.Sqrt,
                                     bias=0.0, scale=1.0), s_s)
                    M(tag + "_sq", s_s)
                if E == "vector":
                    WT(tag + "_sq", s_s)
                    IW(eng.tensor_tensor(VA[0:1, 0:n], MU[0:1, 0:n], INV[0:1, 0:n],
                                        ALU.mult), s_v)
                    IW(eng.tensor_scalar(VA[0:1, 0:n], VA[0:1, 0:n], -1.0, None,
                                        ALU.mult), s_v)
                    IW(eng.tensor_tensor(nm_bf[0:1, 0:n], VA[0:1, 0:n], VA[0:1, 0:n], ALU.max), s_v)
                    IW(eng.tensor_tensor(inv_bf[0:1, 0:n], INV[0:1, 0:n], INV[0:1, 0:n], ALU.max), s_v)
                    M(tag + "_rbf", s_v)
                if E == "tensor":
                    WT(tag + "_rbf", s_v)
                    WT(tag + "_rows", s_ld)
                    I(eng.matmul(PS_B[:, 1024:1024 + n], tb["ones_row"][:],
                                 inv_bf[0:1, 0:n], start=True, stop=True), s_t)
                    M(tag + "_rinv", s_t)
                if E == "vector":
                    WT(tag + "_rinv", s_t)
                    rv = PS_B[:, 1024:1024 + n].unsqueeze(1) \
                        .broadcast_to([128, 8, n])
                    I(eng.tensor_tensor(
                        src_f32[:, 0:4096].rearrange("p (c x) -> p c x", x=512)[:, :, 0:n],
                        src_f32[:, 0:4096].rearrange("p (c x) -> p c x", x=512)[:, :, 0:n],
                        rv, ALU.mult), s_v)
                    M(tag + "_tinv", s_v)
                for c in range(8):
                    bank = (c % 2) * 512
                    if E == "tensor":
                        if c >= 2:
                            WT(f"{tag}_stt{c - 2}", s_v)
                        else:
                            WT(tag + "_tinv", s_v)
                        I(eng.matmul(PS_B[:, bank:bank + n],
                                     rowslot[0:1, c * 128:(c + 1) * 128],
                                     nm_bf[0:1, 0:n], start=True, stop=False), s_t)
                        I(eng.matmul(PS_B[:, bank:bank + n],
                                     rowslot[0:1, 1024 + c * 128:1024 + (c + 1) * 128],
                                     tb["ones_tok"][0:1, 0:n],
                                     start=False, stop=True), s_t)
                        M(f"{tag}_b2{c}", s_t)
                    if E == "vector":
                        WT(f"{tag}_b2{c}", s_t)
                        I(eng.scalar_tensor_tensor(
                            out_ap(c, n),
                            src_f32[:, c * 512:c * 512 + n],
                            vec_c[:, gcol_base + c:gcol_base + c + 1],
                            PS_B[:, bank:bank + n], ALU.mult, ALU.add), s_v)
                        M(f"{tag}_stt{c}", s_v)
                if E == "vector":
                    M(tag + "_out", s_v)
                    M(f"p3row_prev_{ti}_{stage + 1}" if stage == 0 else
                      f"p3row_prev_{ti + 1}_0", s_v)

            for ti, (n, off) in enumerate(zip(TT_N, TT_O)):
                if E == "sync":
                    if ti == 0:
                        WT("p2g6_dq", s_v)
                    if ti > 0:
                        WT(f"p3t{ti - 1}_qkdone", s_s)
                        WT(f"p3t{ti - 1}_vdone", s_s)
                    for c in range(8):
                        I(eng.dma_start(
                            out=xt[:, c * 512:c * 512 + n],
                            in_=x_in[c * 128:(c + 1) * 128, off:off + n]), s_ld, 16)
                    M(f"p3t{ti}_x", s_ld)

                ln_stage(ti, n, 0, xt, 0,  0,
                         lambda c, nn: xt[:, c * 512:c * 512 + nn], "h")
                ln_stage(ti, n, 1, xt, 16, 2048,
                         lambda c, nn: xnt[:, c * 512:c * 512 + nn], "xn")

                for oc in range(4):
                    if E == "sync":
                        if ti == 0 and oc == 0:
                            WT("ag1", s_cc)
                        if ti > 0 or oc > 0:
                            WT(f"p3t{ti}_qk{oc - 1}mm" if oc > 0 else
                               f"p3t{ti - 1}_qk3mm", s_t)
                        I(eng.dma_start(
                            out=wtile[:, 0:1024].rearrange("p (t q) -> p t q", q=128),
                            in_=ag1o.rearrange("(t p) q -> t p q", p=128)
                            [oc * 12:oc * 12 + 8, :, :].transpose([1, 0, 2])),
                          s_ld, 16)
                        M(f"p3t{ti}_qkw{oc}", s_ld)
                    if E == "tensor":
                        WT(f"p3t{ti}_qkw{oc}", s_ld)
                        WT(f"p3t{ti}s1_out", s_v)
                        if ti > 0:
                            WT(f"p3t{ti - 1}_qke{oc}", s_s)
                        for ic in range(8):
                            I(eng.matmul(PS_A[:, oc * 512:oc * 512 + n],
                                         wtile[:, ic * 128:(ic + 1) * 128],
                                         xnt[:, ic * 512:ic * 512 + n],
                                         start=(ic == 0), stop=(ic == 7)), s_t)
                        M(f"p3t{ti}_qk{oc}mm", s_t)
                    if E == "scalar":
                        WT(f"p3t{ti}_qk{oc}mm", s_t)
                        I(eng.copy(qkb[:, oc * 2304 + off: oc * 2304 + off + n],
                                   PS_A[:, oc * 512:oc * 512 + n]), s_s)
                        M(f"p3t{ti}_qke{oc}", s_s)
                if E == "scalar":
                    M(f"p3t{ti}_qkdone", s_s)

                nsub = n // 128
                for ic in range(8):
                    if E == "sync":
                        if ti > 0 or ic > 0:
                            WT(f"p3t{ti}_v{ic - 1}mm" if ic > 0 else
                               f"p3t{ti - 1}_v7mm", s_t)
                        for j in range(4):
                            I(eng.dma_start(
                                out=wvb[:, j * 128:(j + 1) * 128],
                                in_=ag1o[((4 + j) * 12 + ic) * 128:
                                         ((4 + j) * 12 + ic + 1) * 128, :]), s_ld, 16)
                        M(f"p3t{ti}_vw{ic}", s_ld)
                    if E == "tensor":
                        WT(f"p3t{ti}_vw{ic}", s_ld)
                        if ic == 0:
                            WT(f"p3t{ti}s1_stt7", s_v)
                            if ti > 0:
                                WT(f"p3t{ti - 1}_ve{3}", s_s)
                        for s in range(nsub):
                            pa = (PS_B[:, s * 512:(s + 1) * 512] if s < 3
                                  else PS_C[:, 0:512])
                            I(eng.matmul(pa,
                                         xnt[:, ic * 512 + s * 128: ic * 512 + (s + 1) * 128],
                                         wvb[:, 0:512],
                                         start=(ic == 0), stop=(ic == 7)), s_t)
                        M(f"p3t{ti}_v{ic}mm", s_t)
                for s in range(nsub):
                    t18 = off // 128 + s
                    if E == "scalar":
                        WT(f"p3t{ti}_v7mm", s_t)
                        if s > 0:
                            WT(f"p3t{ti}_vsp{s - 1}", s_sp)
                        elif ti > 0:
                            WT(f"p3t{ti - 1}_vsp{TT_N[ti - 1] // 128 - 1}",
                               s_sp)
                        pa = (PS_B[:, s * 512:(s + 1) * 512] if s < 3
                              else PS_C[:, 0:512])
                        I(eng.copy(vstg[:, 0:512], pa), s_s)
                        M(f"p3t{ti}_ve{s}", s_s)
                    if E == "gpsimd":
                        WT(f"p3t{ti}_ve{s}", s_s)
                        I(eng.dma_start(out=v_h.ap()[t18 * 128:(t18 + 1) * 128, :],
                                        in_=vstg[:, 0:512]), s_sp, 16)
                        M(f"p3t{ti}_vsp{s}", s_sp)
                if E == "scalar":
                    M(f"p3t{ti}_vdone", s_s)
            if E == "gpsimd":
                M("p3_vspill", s_sp)

        def emit_p4(E, eng, I, M, WT, WSelf, IW):
            if E == "gpsimd" and KDUMP == "wq5":
                I(eng.dma_start(out=park.ap()[:, 36480:36608],
                                in_=ag2_out.ap()[0:128, :]
                                .bitcast(mybir.dt.uint16)), s_sp, 16)
            qkb = vbf(AR_MID, 4096, 9216)
            expT = vbf(AR_MID, 14848, 1536)
            attno = vbf(AR_SM, 0, 8192)

            def qsl(v, hw, c, colo, n):
                return vbf(AR_BIG, v * 4096, 4096)[:, c * 1024 + colo: c * 1024 + colo + n]

            for hw in range(2):
                krot = vbf(AR_BIG, 12288, 5120)
                vtok = vbf(AR_BIG, 17408, 5120)
                ksw = vbf(AR_BIG, 22528, 1280)
                rt1 = vbf(AR_BIG, 23808, 1280)
                rt2 = vbf(AR_BIG, 25088, 1280)
                if E == "gpsimd" and KDUMP == "sim" and hw == KWSEL // 8:
                    WT("simdump", s_s)
                    I(eng.dma_start(out=park.ap()[:, 26624:27392],
                                    in_=AR_FX[:, 0:384].bitcast(mybir.dt.uint16)),
                      s_sp, 16)
                    M("park_sim", s_sp)
                if E == "gpsimd" and hw == 0 and KDUMP:
                    WT("p3t4_qkdone", s_s)
                    I(eng.dma_start(out=park.ap()[:, 0:9216],
                                    in_=AR_MID[:].bitcast(mybir.dt.uint16)
                                    [:, 4096:13312]), s_sp, 16)
                    M("park_qkb", s_sp)
                    WT("p4h0_rot", s_v)
                    I(eng.dma_start(out=park.ap()[:, 9216:26624],
                                    in_=AR_BIG[:].bitcast(mybir.dt.uint16)
                                    [:, 0:17408]), s_sp, 16)
                    M("park_qrot", s_sp)
                if E == "sync":
                    WT("p3_vspill", s_sp)
                    if hw > 0:
                        WT(f"p4h{hw - 1}_bvdone", s_t)
                    I(eng.dma_start(out=vtok[:, 0:5120]
                                    .rearrange("p (t q) -> p t q", q=512),
                                    in_=v_h.ap()[hw * 1024:hw * 1024 + 1280, :]
                                    .rearrange("(t p) q -> t p q", p=128)
                                    .transpose([1, 0, 2])), s_ld, 16)
                    M(f"p4h{hw}_vld", s_ld)
                for c in range(4):
                    tag = f"p4h{hw}r{c}"
                    kcol = 2304 * c + hw * 1024
                    if E == "scalar":
                        if hw > 0:
                            WT(f"p4h{hw - 1}_bvdone", s_t)
                        if c > 0:
                            WT(f"p4h{hw}r{c - 1}_done", s_v)
                        I(eng.copy(ksw[0:32, 0:1280], qkb[32:64, kcol:kcol + 1280]), s_s)
                        I(eng.copy(ksw[32:64, 0:1280], qkb[0:32, kcol:kcol + 1280]), s_s)
                        I(eng.copy(ksw[64:96, 0:1280], qkb[96:128, kcol:kcol + 1280]), s_s)
                        I(eng.copy(ksw[96:128, 0:1280], qkb[64:96, kcol:kcol + 1280]), s_s)
                        M(tag + "_sw", s_s)
                    if E == "vector":
                        WT(tag + "_sw", s_s)
                        def rep(tn, nw):
                            return tb[tn][:].unsqueeze(1) \
                                .broadcast_to([128, nw, 128])
                        kk = qkb[:, kcol:kcol + 1280].rearrange("p (w x) -> p w x", x=128)
                        ks = ksw[:, 0:1280].rearrange("p (w x) -> p w x", x=128)
                        I(eng.tensor_tensor(rt1[:, 0:1280].rearrange("p (w x) -> p w x", x=128),
                                            kk, rep("ak", 10), ALU.mult), s_v)
                        I(eng.tensor_tensor(rt2[:, 0:1280].rearrange("p (w x) -> p w x", x=128),
                                            ks, rep("bk", 10), ALU.mult), s_v)
                        I(eng.tensor_tensor(krot[:, c * 1280:(c + 1) * 1280]
                                            .rearrange("p (w x) -> p w x", x=128),
                                            rt1[:, 0:1280].rearrange("p (w x) -> p w x", x=128),
                                            rt2[:, 0:1280].rearrange("p (w x) -> p w x", x=128),
                                            ALU.add), s_v)
                        qq = qkb[:, kcol + 128:kcol + 1152].rearrange("p (w x) -> p w x", x=128)
                        qs = ksw[:, 128:1152].rearrange("p (w x) -> p w x", x=128)
                        for vi, o in enumerate((0, 128, 256)):
                            I(eng.tensor_tensor(rt1[:, 0:1024].rearrange("p (w x) -> p w x", x=128),
                                                qq, rep(f"aq{o}", 8), ALU.mult), s_v)
                            I(eng.tensor_tensor(rt2[:, 0:1024].rearrange("p (w x) -> p w x", x=128),
                                                qs, rep(f"bq{o}", 8), ALU.mult), s_v)
                            I(eng.tensor_tensor(qsl(vi, hw, c, 0, 1024)
                                                .rearrange("p (w x) -> p w x", x=128),
                                                rt1[:, 0:1024].rearrange("p (w x) -> p w x", x=128),
                                                rt2[:, 0:1024].rearrange("p (w x) -> p w x", x=128),
                                                ALU.add), s_v)
                        M(tag + "_done", s_v)
                if E == "vector":
                    M(f"p4h{hw}_rot", s_v)

                for wl in range(8):
                    w = hw * 8 + wl
                    mneg = ("mneg_w0" if w == 0 else
                            "mneg_w15" if w == 15 else "mneg_mid")
                    for hg in range(2):
                        tagA = f"p4aw{w}g{hg}"
                        if E == "tensor":
                            WT(f"p4h{hw}_rot", s_v)
                            WT(f"p4h{hw}_vld", s_ld)
                            if wl > 0:
                                WT(f"p4aw{w - 1}g{hg}_exp", s_s)
                            elif hw > 0:
                                WT(f"p4bw{hw * 8 - 1}g1_oev", s_s)
                            for hh in range(4):
                                h = hg * 4 + hh
                                c, base = h // 2, (h % 2) * 64
                                for ci, vo in ((0, 2), (1, 1), (2, 0)):
                                    ps = (PS_A if hg == 0 else PS_B if hh < 3 else PS_C)
                                    col = (hh * 512 if hg == 0 else
                                           hh * 512 if hh < 3 else 0) + ci * 128
                                    I(eng.matmul(
                                        ps[:, col:col + 128],
                                        qsl(vo, hw, c, wl * 128, 128)[base:base + 64, :],
                                        krot[base:base + 64,
                                             c * 1280 + (wl + ci) * 128:
                                             c * 1280 + (wl + ci + 1) * 128],
                                        start=(ci == 0), stop=(ci == 2)), s_t)
                            M(tagA + "_mm", s_t)
                        if E == "scalar" and KDUMP == "sim" and w == KWSEL and hg == KHSEL // 4:
                            WT(tagA + "_mm", s_t)
                            hh0 = KHSEL % 4
                            ps0 = (PS_A if KHSEL // 4 == 0 else PS_B if hh0 < 3 else PS_C)
                            col0 = (hh0 * 512 if KHSEL // 4 == 0 or hh0 < 3 else 0)
                            I(eng.copy(AR_FX[:, 0:384], ps0[:, col0:col0 + 384]), s_s)
                            M("simdump", s_s)
                        if E == "vector":
                            WT(tagA + "_mm", s_t)
                            if KDUMP == "sim" and w == KWSEL and hg == KHSEL // 4:
                                WT("simdump", s_s)
                            for hh in range(4):
                                ps = (PS_A if hg == 0 else PS_B if hh < 3 else PS_C)
                                col = (hh * 512 if hg == 0 or hh < 3 else 0)
                                pa = ps[:, col:col + 384]
                                I(eng.tensor_tensor(pa, pa, tb[mneg][:], ALU.add), s_v)
                                I(eng.tensor_reduce(
                                    Mall[:, w * 8 + hg * 4 + hh: w * 8 + hg * 4 + hh + 1]
                                    .unsqueeze(2),
                                    pa.rearrange("p (a x) -> p a x", a=1),
                                    AX.X, ALU.max, negate=True), s_v)
                            M(tagA + "_msk", s_v)
                        if E == "scalar":
                            WT(tagA + "_msk", s_v)
                            for hh in range(4):
                                h = hg * 4 + hh
                                ps = (PS_A if hg == 0 else PS_B if hh < 3 else PS_C)
                                col = (hh * 512 if hg == 0 or hh < 3 else 0)
                                I(eng.activation(expT[:, 0:384], ps[:, col:col + 384],
                                                 AF.Exp, bias=Mall[:, w * 8 + h:w * 8 + h + 1],
                                                 scale=1.0,
                                                 accum_out=Sall[:, w * 8 + h:w * 8 + h + 1]),
                                  s_s)
                            M(tagA + "_exp", s_s)
                if E == "tensor":
                    M(f"p4h{hw}_simdone", s_t)

                tagB = f"p4h{hw}b"
                if E == "scalar":
                    WT(f"p4aw{hw * 8 + 7}g1_exp", s_s)
                    I(eng.activation(biasM[:, hw * 64:hw * 64 + 64],
                                     Sall[:, hw * 64:hw * 64 + 64], AF.Ln,
                                     bias=0.0, scale=2.0), s_s)
                    M(tagB + "_ln", s_s)
                if E == "vector":
                    WT(tagB + "_ln", s_s)
                    IW(eng.tensor_scalar(biasM[:, hw * 64:hw * 64 + 64],
                                        biasM[:, hw * 64:hw * 64 + 64],
                                        -1.0, LN2, ALU.mult, ALU.add), s_v)
                    IW(eng.tensor_tensor(biasM[:, hw * 64:hw * 64 + 64],
                                        biasM[:, hw * 64:hw * 64 + 64],
                                        Mall[:, hw * 64:hw * 64 + 64], ALU.add), s_v)
                    M(tagB + "_v", s_v)
                if E == "tensor":
                    WT(tagB + "_v", s_v)
                    I(eng.transpose(PS_C[0:64, 0:128], biasM[:, hw * 64:hw * 64 + 64],
                                    tident[:]), s_t)
                    M(tagB + "_tr", s_t)
                if E == "scalar":
                    WT(tagB + "_tr", s_t)
                    I(eng.copy(vbf(AR_BIG, 26368, 128)[0:64, :], PS_C[0:64, 0:128]), s_s)
                    M(tagB + "_ev", s_s)
                if E == "gpsimd":
                    WT(tagB + "_ev", s_s)
                    I(eng.dma_start(out=nbias_bnc.ap()[hw * 64:hw * 64 + 64, :],
                                    in_=vbf(AR_BIG, 26368, 128)[0:64, :]), s_sp, 16)
                    M(tagB + "_bnc", s_sp)

                for wl in range(8):
                    w = hw * 8 + wl
                    m01l = "m01_lw0" if w == 0 else "m01_l"
                    m01r = "m01_rw15" if w == 15 else "m01_r"
                    if E == "gpsimd":
                        WT(f"p4h{hw}b_bnc", s_sp)
                        if wl > 0 or hw > 0:
                            WT(f"p4bw{w - 1 if wl else hw * 8 - 1}g1_mm", s_t)
                        I(eng.dma_start(out=rowslot[0:1, 0:1024],
                                        in_=nbias_bnc.ap()
                                        .rearrange("a b -> (a b)")
                                        .unsqueeze(0)[0:1, w * 1024:(w + 1) * 1024]),
                          s_sp, 16)
                        M(f"p4bw{w}_rows", s_sp)
                    for hg in range(2):
                        tagb = f"p4bw{w}g{hg}"
                        if E == "tensor":
                            WT(f"p4bw{w}_rows", s_sp)
                            if wl > 0 or hw > 0 or hg > 0:
                                ptag = (f"p4bw{w}g0" if hg else
                                        f"p4bw{w - 1 if wl else hw * 8 - 1}g1")
                                WT(ptag + "_exp", s_s)
                            else:
                                WT(f"p4aw{hw * 8 + 7}g1_exp", s_s)
                            for hh in range(4):
                                h = hg * 4 + hh
                                c, base = h // 2, (h % 2) * 64
                                for ci, vo in ((0, 2), (1, 1), (2, 0)):
                                    I(eng.matmul(
                                        PS_A[:, hh * 512 + ci * 128: hh * 512 + (ci + 1) * 128],
                                        krot[base:base + 64,
                                             c * 1280 + (wl + ci) * 128:
                                             c * 1280 + (wl + ci + 1) * 128],
                                        qsl(vo, hw, c, wl * 128, 128)[base:base + 64, :],
                                        start=(ci == 0), stop=False), s_t)
                                for ci in range(3):
                                    I(eng.matmul(
                                        PS_A[:, hh * 512 + ci * 128: hh * 512 + (ci + 1) * 128],
                                        tb["ones_row"][:],
                                        rowslot[0:1, h * 128:(h + 1) * 128],
                                        start=False, stop=(ci == 2)), s_t)
                            M(tagb + "_mm", s_t)
                        if E == "scalar":
                            WT(tagb + "_mm", s_t)
                            I(eng.activation(
                                expT[:, 0:1536].rearrange("p (h x) -> p h x", x=384),
                                PS_A[:, 0:2048].rearrange("p (h x) -> p h x", x=512)[:, :, 0:384],
                                AF.Exp, bias=0.0, scale=1.0), s_s)
                            M(tagb + "_exp", s_s)
                        if E == "vector":
                            WT(tagb + "_exp", s_s)
                            ex = expT[:, 0:1536].rearrange("p (h c x) -> p h c x", c=3, x=128)
                            for ci, mn in ((0, m01l), (2, m01r)):
                                I(eng.tensor_tensor(
                                    ex[:, :, ci, :],
                                    ex[:, :, ci, :],
                                    tb[mn][:].unsqueeze(1)
                                    .broadcast_to([128, 4, 128]), ALU.mult), s_v)
                            M(tagb + "_mask", s_v)
                        if E == "tensor":
                            WT(tagb + "_mask", s_v)
                            if wl > 0 or hw > 0 or hg > 0:
                                ptag = (f"p4bw{w}g0" if hg else
                                        f"p4bw{w - 1 if wl else hw * 8 - 1}g1")
                                WT(ptag + "_oev", s_s)
                            for hh in range(4):
                                h = hg * 4 + hh
                                for ci in range(3):
                                    I(eng.matmul(
                                        PS_C[0:64, hh * 128:(hh + 1) * 128],
                                        vtok[:, (wl + ci) * 512 + h * 64:
                                             (wl + ci) * 512 + (h + 1) * 64],
                                        expT[:, hh * 384 + ci * 128: hh * 384 + (ci + 1) * 128],
                                        start=(ci == 0 and hh == 0),
                                        stop=(ci == 2 and hh == 3)), s_t)
                            M(tagb + "_bv", s_t)
                        if E == "scalar":
                            WT(tagb + "_bv", s_t)
                            for hh in range(4):
                                h = hg * 4 + hh
                                c2 = h // 2
                                pb = (h % 2) * 64
                                I(eng.copy(attno[pb:pb + 64,
                                                 c2 * 2048 + wl * 128 + hw * 1024:
                                                 c2 * 2048 + wl * 128 + hw * 1024 + 128],
                                           PS_C[0:64, hh * 128:(hh + 1) * 128]), s_s)
                            M(tagb + "_oev", s_s)
                if E == "tensor":
                    M(f"p4h{hw}_bvdone", s_t)
            if E == "scalar":
                M("p4_attno", s_s)

        def emit_p5(E, eng, I, M, WT, WSelf, IW):
            if E == "gpsimd" and KDUMP == "wq5":
                I(eng.dma_start(out=park.ap()[:, 36608:36736],
                                in_=ag2_out.ap()[0:128, :]
                                .bitcast(mybir.dt.uint16)), s_sp, 16)
            attno = vbf(AR_SM, 0, 8192)
            if E == "gpsimd" and KDUMP:
                WT("p4_attno", s_s)
                I(eng.dma_start(out=park.ap()[:, 27392:35584],
                                in_=AR_SM[:].bitcast(mybir.dt.uint16)[:, 0:8192]),
                  s_sp, 16)
                M("park_attno", s_sp)
            def xres5(f):
                return AR_FX[:, (f % 2) * 512:(f % 2) * 512 + 512]

            def x2t5(f):
                return AR_FX[:, 1024 + (f % 2) * 512:1536 + (f % 2) * 512]

            def bftA(f):
                return vbf(AR_MID, (f % 2) * 512, 512)[:, 0:512]

            def bftB(f):
                return vbf(AR_MID, 1024 + (f % 2) * 512, 512)[:, 0:512]

            def t5(f):
                return f"p5g{f // 8}o{f % 8}"

            def emit_stats5(f):
                o = f % 8
                if o == 0 and f // 8 > 0:
                    WT(f"p6r{f // 8 - 1}_v1", s_v)
                WT(t5(f) + "_c1", s_s)
                I(eng.matmul(PS_B[0:1, 0:512], tb["ones_col"][:],
                             bftA(f), start=(o == 0), stop=(o == 7)), s_t)
                M(t5(f) + "_mm1", s_t)
                WT(t5(f) + "_c", s_s)
                I(eng.matmul(PS_B[0:1, 512:1024], tb["ones_col"][:],
                             bftB(f), start=(o == 0), stop=(o == 7)), s_t)
                M(t5(f) + "_smm", s_t)

            for gi, g in enumerate(G4):
                for oc in range(8):
                    tag = f"p5g{gi}o{oc}"
                    f5 = gi * 8 + oc
                    wh5 = (f5 % 2) * 512
                    if E == "sync":
                        WT("p4_attno", s_s)
                        if KDUMP == "sim" and gi == 0 and oc == 0:
                            WT("park_sim", s_sp)
                        if f5 >= 2:
                            WT(t5(f5 - 2) + "_res", s_v)
                            WT(t5(f5 - 2) + "_mm", s_t)
                        I(eng.dma_start(
                            out=wtile[:, wh5:wh5 + 512]
                            .rearrange("p (t q) -> p t q", q=128),
                            in_=ag1o.rearrange("(t p) q -> t p q", p=128)
                            [oc * 12 + 8:oc * 12 + 12, :, :].transpose([1, 0, 2])),
                          s_ld, 16)
                        I(eng.dma_start(
                            out=xres5(f5),
                            in_=x_in[oc * 128:(oc + 1) * 128,
                                     HALO + g:HALO + g + 512]), s_ld, 16)
                        M(tag + "_ld", s_ld)
                    if E == "tensor":
                        WT(tag + "_ld", s_ld)
                        if f5 >= 2:
                            WT(t5(f5 - 2) + "_res", s_v)
                        for ic in range(4):
                            I(eng.matmul(PS_A[:, wh5:wh5 + 512],
                                         wtile[:, wh5 + ic * 128:
                                               wh5 + (ic + 1) * 128],
                                         attno[:, ic * 2048 + g: ic * 2048 + g + 512],
                                         start=(ic == 0), stop=(ic == 3)), s_t)
                        M(tag + "_mm", s_t)
                        if oc > 0:
                            emit_stats5(f5 - 1)
                    if E == "scalar" and KDUMP == "x2e" and gi == 3 and oc == 7:
                        WT(tag + "_mm", s_t)
                        I(eng.copy(AR_FX[:, 2048:2560], PS_A[:, wh5:wh5 + 512]),
                          s_s)
                        M("psA_cap", s_s)
                    if E == "vector":
                        WT(tag + "_mm", s_t)
                        if KDUMP == "x2e" and gi == 3 and oc == 7:
                            WT("psA_cap", s_s)
                        if f5 >= 2:
                            WT(t5(f5 - 2) + "_sp", s_sp)
                            WT(t5(f5 - 2) + "_c", s_s)
                        I(eng.tensor_tensor(x2t5(f5), PS_A[:, wh5:wh5 + 512],
                                            xres5(f5), ALU.add), s_v)
                        M(tag + "_res", s_v)
                    if E == "gpsimd":
                        WT(tag + "_res", s_v)
                        if KDUMP == "x2e" and gi == 3 and oc == 7:
                            I(eng.dma_start(out=park.ap()[:, 37792:38816],
                                            in_=AR_FX[:].bitcast(mybir.dt.uint16)
                                            [:, 4096:5120]), s_sp, 16)
                        I(eng.dma_start(out=x2_h.ap()[oc * 128:(oc + 1) * 128,
                                                      g:g + 512],
                                        in_=x2t5(f5)), s_sp, 16)
                        M(tag + "_sp", s_sp)
                    if E == "scalar":
                        WT(tag + "_res", s_v)
                        if f5 >= 2:
                            WT(t5(f5 - 2) + "_mm1", s_t)
                            WT(t5(f5 - 2) + "_smm", s_t)
                        I(eng.copy(bftA(f5), x2t5(f5)), s_s)
                        M(tag + "_c1", s_s)
                        I(eng.activation(bftB(f5), x2t5(f5),
                                         AF.Square, bias=0.0, scale=1.0), s_s)
                        M(tag + "_c", s_s)
                if E == "tensor":
                    emit_stats5(gi * 8 + 7)
                    M(f"p5g{gi}_stats", s_t)
                emit_ln3_rows(E, eng, I, M, WT, IW, gi)
            if E == "gpsimd":
                if KDUMP == "x2e":
                    WT("psA_cap", s_s)
                    WSelf(s_sp)
                    I(eng.dma_start(out=park.ap()[:, 35744:36768],
                                    in_=x2_h.ap()[896:1024, 1536:2048]
                                    .bitcast(mybir.dt.uint16)), s_sp, 16)
                    I(eng.dma_start(out=park.ap()[:, 36768:37792],
                                    in_=AR_FX[:].bitcast(mybir.dt.uint16)
                                    [:, 2048:3072]), s_sp, 16)
                    M("park_x2e", s_sp)
                M("p5_x2", s_sp)

        def emit_ln3_rows(E, eng, I, M, WT, IW, gi):
            MU3 = AR_ROW[0:1, 512:1024]
            V3 = AR_ROW[0:1, 1024:1536]
            tag = f"p6r{gi}"
            if E == "vector":
                WT(f"p5g{gi}_stats", s_t)
                IW(eng.tensor_scalar(MU3, PS_B[0:1, 0:512], 1.0 / 1024.0, None,
                                    ALU.mult), s_v)
                IW(eng.tensor_scalar(V3, PS_B[0:1, 512:1024], 1.0 / 1024.0, None,
                                    ALU.mult), s_v)
                IW(eng.tensor_tensor(AR_ROW[0:1, 0:512], MU3, MU3, ALU.mult), s_v)
                IW(eng.tensor_tensor(V3, V3, AR_ROW[0:1, 0:512], ALU.subtract), s_v)
                IW(eng.tensor_scalar(V3, V3, LN_EPS, None, ALU.add), s_v)
                IW(eng.reciprocal(V3, V3), s_v)
                M(tag + "_v1", s_v)
            if E == "scalar":
                WT(tag + "_v1", s_v)
                I(eng.activation(V3, V3, AF.Sqrt, bias=0.0, scale=1.0), s_s)
                M(tag + "_sq", s_s)
            if E == "vector":
                WT(tag + "_sq", s_s)
                IW(eng.tensor_tensor(AR_ROW[0:1, 0:512], MU3, V3, ALU.mult), s_v)
                IW(eng.tensor_scalar(AR_ROW[0:1, 0:512], AR_ROW[0:1, 0:512],
                                    -1.0, None, ALU.mult), s_v)
                IW(eng.tensor_tensor(AR_ROW[:].bitcast(BF16)[0:1, 5120 + gi * 512:
                                                          5120 + gi * 512 + 512],
                                  AR_ROW[0:1, 0:512],
                                  AR_ROW[0:1, 0:512], ALU.max), s_v)
                IW(eng.tensor_tensor(AR_ROW[:].bitcast(BF16)[0:1, 3072 + gi * 512:
                                                          3072 + gi * 512 + 512],
                                  V3,
                                  V3, ALU.max), s_v)
                M(tag + "_done", s_v)

        def emit_p6(E, eng, I, M, WT, WSelf, IW, row_of):
            if E == "gpsimd" and KDUMP == "wq5":
                I(eng.dma_start(out=park.ap()[:, 36736:36864],
                                in_=ag2_out.ap()[0:128, :]
                                .bitcast(mybir.dt.uint16)), s_sp, 16)
            h2stg = vf16(AR_SM, 0, 1024)
            x2c = AR_FX[:, 0:1024]
            if E == "vector":
                I(eng.memset(amaxA[:, 0:1024], 0.0), s_v)
                M("p6_amz", s_v)
            for gi, g in enumerate(G4):
                if E == "sync":
                    WT(f"p6r{gi}_done", s_v)
                    I(eng.dma_start(out=rowslot[0:1, 0:2048],
                                    in_=rows_bnc.ap().rearrange("a b -> (a b)")
                                    .unsqueeze(0)[0:1, 4096:6144]), s_ld, 16)
                    M(f"p6g{gi}_rows", s_ld)
                if E == "tensor":
                    WT(f"p6r{gi}_done", s_v)
                    if gi == 0:
                        WT("p6r3_v1", s_v)
                    else:
                        WT(f"p6g{gi - 1}c7_stt", s_v)
                    I(eng.matmul(PS_B[:, 1024:1536],
                                 tb["ones_row"][:],
                                 AR_ROW[:].bitcast(BF16)[0:1, 3072 + gi * 512:
                                                         3072 + gi * 512 + 512],
                                 start=True, stop=True), s_t)
                    M(f"p6g{gi}_rinv", s_t)
                for c in range(8):
                    tag = f"p6g{gi}c{c}"
                    flat6 = gi * 8 + c
                    xsl6 = (c % 2) * 512
                    if E == "sync":
                        WT(f"p5g{gi}o{c}_sp", s_sp)
                        if flat6 <= 1:
                            WT("p5g3o7_res", s_v)
                            WT("p5g3o7_c", s_s)
                            WT("p5_x2", s_sp)
                        else:
                            WT(f"p6g{(flat6 - 2) // 8}c{(flat6 - 2) % 8}_stt",
                               s_v)
                        I(eng.dma_start(out=x2c[:, xsl6:xsl6 + 512],
                                        in_=x2_h.ap()[c * 128:(c + 1) * 128, g:g + 512]),
                          s_ld, 16)
                        M(tag + "_x2", s_ld)
                    if E == "tensor":
                        WT(f"p6g{gi}_rows", s_ld)
                        if c >= 2:
                            WT(f"p6g{gi}c{c - 2}_stt", s_v)
                        bank = (c % 2) * 512
                        I(eng.matmul(PS_B[:, bank:bank + 512],
                                     rowslot[0:1, c * 128:(c + 1) * 128],
                                     AR_ROW[:].bitcast(BF16)[0:1, 5120 + gi * 512:
                                                             5120 + gi * 512 + 512],
                                     start=True, stop=False), s_t)
                        I(eng.matmul(PS_B[:, bank:bank + 512],
                                     rowslot[0:1, 1024 + c * 128:1024 + (c + 1) * 128],
                                     tb["ones_tok"][0:1, 0:512],
                                     start=False, stop=True), s_t)
                        M(tag + "_b2", s_t)
                    if E == "vector":
                        WT(tag + "_x2", s_ld)
                        WT(f"p6g{gi}_rinv", s_t)
                        WT(tag + "_b2", s_t)
                        flat6 = gi * 8 + c
                        if flat6 >= 2:
                            WT(f"p6g{(flat6 - 2) // 8}c{(flat6 - 2) % 8}_sp",
                               s_sp)
                        I(eng.tensor_tensor(x2c[:, xsl6:xsl6 + 512],
                                            x2c[:, xsl6:xsl6 + 512],
                                            PS_B[:, 1024:1536], ALU.mult), s_v)
                        bank = (c % 2) * 512
                        I(eng.scalar_tensor_tensor(
                            h2stg[:, (c % 2) * 512:(c % 2) * 512 + 512],
                            x2c[:, xsl6:xsl6 + 512], vec_c[:, 32 + c:32 + c + 1],
                            PS_B[:, bank:bank + 512], ALU.mult, ALU.add), s_v)
                        am6 = gi * 8 + c
                        I(eng.tensor_reduce(
                            amaxA[:, am6:am6 + 1].unsqueeze(2),
                            h2stg[:, (c % 2) * 512:(c % 2) * 512 + 512]
                            .rearrange("p (a x) -> p a x", a=1),
                            AX.X, ALU.max, apply_absolute_value=True), s_v)
                        M(tag + "_stt", s_v)
                    if E == "gpsimd":
                        WT(tag + "_stt", s_v)
                        I(eng.dma_start(out=h2_h.ap()[c * 128:(c + 1) * 128, g:g + 512],
                                        in_=h2stg[:, (c % 2) * 512:(c % 2) * 512 + 512]),
                          s_sp, 16)
                        M(tag + "_sp", s_sp)
            if KP6 < 2:
                return
            emit_scale_ag(E, eng, I, M, WT, WSelf, IW, 0, "p6sx", amaxA, 0, [("sx", 0)])
            if KP6 < 3:
                return
            for c in range(8):
                tag = f"p6xq{c}"
                ginq = vf16(AR_SM, (c % 2) * 2048, 2048)
                tfq = (vf32(AR_SM, 4096, 2048) if c % 2 == 0
                       else AR_FX[:, 0:2048])
                if E == "sync":
                    WT("p6g3c7_sp", s_sp)
                    if c >= 2:
                        WT(f"p6xq{c - 2}_t", s_s)
                    I(eng.dma_start(out=ginq[:, 0:2048],
                                    in_=h2_h.ap()[c * 128:(c + 1) * 128, :]), s_ld, 16)
                    M(tag + "_ld", s_ld)
                if E == "scalar":
                    WT(tag + "_ld", s_ld)
                    WT("p6sx_cols", s_v)
                    if c >= 2:
                        WT(f"p6xq{c - 2}_q", s_v)
                    if c <= 1:
                        WT("p6g3c7_stt", s_v)
                    I(eng.activation(tfq[:, 0:2048], ginq[:, 0:2048],
                                     AF.Identity, bias=cmag[:, 0:1],
                                     scale=scol[:, 1:2]), s_s)
                    M(tag + "_t", s_s)
                if E == "vector":
                    WT(tag + "_t", s_s)
                    I(eng.tensor_scalar(vbf(AR_MID, c * 2048, 2048)[:, 0:2048],
                                        tfq[:, 0:2048],
                                        MAGIC, scol[:, 0:1], ALU.subtract, ALU.mult),
                      s_v)
                    M(tag + "_q", s_v)
            if E == "vector":
                M("p6_xq", s_v)

        def emit_scale_ag(E, eng, I, M, WT, WSelf, IW, agi, tag, acc, acc_off, specs):
            nsc = len(specs)
            if E == "vector":
                for si, (nm, sidx) in enumerate(specs):
                    IW(eng.tensor_reduce(scol[:, 15:16].unsqueeze(2),
                                        acc[:, acc_off + si * 1024: acc_off + (si + 1) * 1024]
                                        .rearrange("p (a x) -> p a x", a=1),
                                        AX.X, ALU.max, apply_absolute_value=True), s_v)
                    IW(eng.tensor_tensor(biasM[:, 120 + si:121 + si], scol[:, 15:16], scol[:, 15:16], ALU.max), s_v)
                M(tag + "_cols0", s_v)
            if E == "tensor":
                WT(tag + "_cols0", s_v)
                I(eng.transpose(PS_C[0:nsc, 0:128], biasM[:, 120:120 + nsc],
                                tident[:]), s_t)
                M(tag + "_tr", s_t)
            if E == "vector":
                WT(tag + "_tr", s_t)
                IW(eng.tensor_scalar(biasM[0:nsc, 0:128], PS_C[0:nsc, 0:128], 0.0, None, ALU.add), s_v)
                IW(eng.tensor_reduce(wblk[0:nsc, 0:1].unsqueeze(2),
                                    biasM[0:nsc, 0:128]
                                    .rearrange("p (a x) -> p a x", a=1),
                                    AX.X, ALU.max), s_v)
                IW(eng.memset(scal[0:1, 0:8], 0.0), s_v)
                M(tag + "_part", s_v)
            if E == "gpsimd":
                WT(tag + "_part", s_v)
                I(eng.dma_start(out=agO_in[agi].ap(), in_=scal[0:1, 0:8]), s_sp, 16)
                WSelf(s_sp)
                I(eng.dma_start(out=agO_in[agi].ap()[0:1, 0:nsc].transpose([1, 0]),
                                in_=wblk[0:nsc, 0:1]), s_sp, 16)
                WSelf(s_sp)
                I(eng.collective_compute("AllGather", ALU.bypass,
                                         replica_groups=[list(range(8))],
                                         ins=[agO_in[agi].ap()],
                                         outs=[agO_out[agi].ap()]), s_cc)
                WSelf(s_cc)
                I(eng.dma_start(out=scal[0:1, 8:8 + 64],
                                in_=agO_out[agi].ap().rearrange("a b -> (a b)")
                                .unsqueeze(0)), s_sp, 16)
                M(tag + "_ag", s_sp)
            if E == "vector":
                WT(tag + "_ag", s_sp)
                for si, (nm, sidx) in enumerate(specs):
                    IW(eng.tensor_reduce(scal[0:1, si:si + 1].unsqueeze(2),
                                        scal[0:1, 8:72]
                                        .rearrange("p (r k) -> p r k", k=8)
                                        .transpose([0, 2, 1])[0:1, si:si + 1, :],
                                        AX.X, ALU.max), s_v)
                    IW(eng.tensor_scalar(scal[0:1, si:si + 1], scal[0:1, si:si + 1],
                                        1.0 / QMAX, 1e-12, ALU.mult, ALU.max), s_v)
                    IW(eng.reciprocal(scal[0:1, 32 + si:33 + si],
                                     scal[0:1, si:si + 1]), s_v)
                M(tag + "_sc", s_v)
            if E == "tensor":
                WT(tag + "_sc", s_v)
                for si, (nm, sidx) in enumerate(specs):
                    I(eng.matmul(PS_C[:, 384 + si * 2:384 + si * 2 + 1], tb_or32[:],
                                 scal[0:1, si:si + 1], start=True, stop=True), s_t)
                    I(eng.matmul(PS_C[:, 384 + si * 2 + 1:384 + si * 2 + 2], tb_or32[:],
                                 scal[0:1, 32 + si:33 + si], start=True, stop=True), s_t)
                M(tag + "_rep", s_t)
            if E == "vector":
                WT(tag + "_rep", s_t)
                for si, (nm, sidx) in enumerate(specs):
                    IW(eng.tensor_scalar(scol[:, sidx:sidx + 1],
                                      PS_C[:, 384 + si * 2:384 + si * 2 + 1],
                                      0.0, None, ALU.add), s_v)
                    IW(eng.tensor_scalar(scol[:, sidx + 1:sidx + 2],
                                      PS_C[:, 384 + si * 2 + 1:384 + si * 2 + 2],
                                      0.0, None, ALU.add), s_v)
                M(tag + "_cols", s_v)

        def emit_p7(E, eng, I, M, WT, WSelf, IW):
            if E == "gpsimd" and KDUMP == "wq5":
                I(eng.dma_start(out=park.ap()[:, 36864:36992],
                                in_=ag2_out.ap()[0:128, :]
                                .bitcast(mybir.dt.uint16)), s_sp, 16)
            upres = vf16(AR_BIG, 0, 49152)
            gstg = vf16(AR_SM, 0, 1024)
            if E == "vector":
                WT("p6_xq", s_v)
                I(eng.memset(amaxA[:, 0:2048], 0.0), s_v)
                M("p7_amz", s_v)
            for which in (0, 1):
                for oc in range(NHC):
                    tag = f"p7w{which}o{oc}"
                    flat = which * NHC + oc
                    half = (flat % 2) * 1024
                    ptag = f"p7w{which}o{oc - 1}" if oc > 0 else \
                        f"p7w{1 - which}o{NHC - 1}"
                    if E == "sync":
                        if which == 0 and oc == 0:
                            WT("ag2", s_cc)
                            WT("p6_xq", s_v)
                            if KDUMP:
                                WT("park_attno", s_sp)
                        elif flat >= 2:
                            pp = flat - 2
                            WT(f"p7w{pp // NHC}o{pp % NHC}_mm", s_t)
                        ti0 = wgud_tidx(1 - which, 0, oc)
                        I(eng.dma_start(
                            out=wtile[:, half:half + 1024]
                            .rearrange("p (t q) -> p t q", q=128),
                            in_=ag2o.rearrange("(t p) q -> t p q", p=128)
                            [ti0:ti0 + 8, :, :].transpose([1, 0, 2])),
                          s_ld, 16)
                        M(tag + "_w", s_ld)
                    for tg in range(4):
                        g = G4[tg]
                        if E == "tensor":
                            if tg == 0:
                                WT(tag + "_w", s_ld)
                            if flat > 0:
                                WT(f"{ptag}_e{tg}", s_s)
                            for ic in range(8):
                                I(eng.matmul(PS_A[:, tg * 512:(tg + 1) * 512],
                                             wtile[:, half + ic * 128:
                                                   half + (ic + 1) * 128],
                                             vbf(AR_MID, ic * 2048 + g, 512)[:, 0:512],
                                             start=(ic == 0), stop=(ic == 7)), s_t)
                            M(f"{tag}_m{tg}", s_t)
                        if E == "scalar":
                            WT(f"{tag}_m{tg}", s_t)
                            if which == 0:
                                I(eng.copy(upres[:, oc * 2048 + g: oc * 2048 + g + 512],
                                           PS_A[:, tg * 512:(tg + 1) * 512]), s_s)
                            else:
                                if tg >= 2:
                                    WT(f"{tag}_sp{tg - 2}", s_sp)
                                elif tg == 0 and oc > 0:
                                    WT(f"p7w1o{oc - 1}_sp3", s_sp)
                                I(eng.copy(gstg[:, (tg % 2) * 512:(tg % 2) * 512 + 512],
                                           PS_A[:, tg * 512:(tg + 1) * 512]), s_s)
                            M(f"{tag}_e{tg}", s_s)
                        if E == "gpsimd" and which == 1:
                            WT(f"{tag}_e{tg}", s_s)
                            I(eng.dma_start(out=gate_h.ap()[oc * 128:(oc + 1) * 128,
                                                            g:g + 512],
                                            in_=gstg[:, (tg % 2) * 512:(tg % 2) * 512 + 512]),
                              s_sp, 16)
                            M(f"{tag}_sp{tg}", s_sp)
                        if E == "vector":
                            WT(f"{tag}_e{tg}", s_s)
                            srcv = (upres[:, oc * 2048 + g: oc * 2048 + g + 512]
                                    if which == 0 else
                                    gstg[:, (tg % 2) * 512:(tg % 2) * 512 + 512])
                            am7 = which * 1024 + oc * 4 + tg
                            I(eng.tensor_reduce(
                                amaxA[:, am7:am7 + 1].unsqueeze(2),
                                srcv.rearrange("p (a x) -> p a x", a=1),
                                AX.X, ALU.max, apply_absolute_value=True), s_v)
                            M(f"{tag}_am{tg}", s_v)
                    if E == "tensor":
                        M(tag + "_mm", s_t)
                    if E == "scalar":
                        M(tag + "_ev", s_s)
                    if E == "vector":
                        M(tag + "_am", s_v)
            emit_scale_ag(E, eng, I, M, WT, WSelf, IW, 1, "p7s", amaxA, 0,
                          [("su", 2), ("sg", 4)])

        def emit_p8(E, eng, I, M, WT, WSelf, IW):
            def ginb(oc):
                return vf16(AR_SM, (oc % 2) * 2048, 2048)
            tf = AR_FX[:, 0:2048]
            g2o = vf16(AR_SM, 4096, 2048)
            if E == "vector":
                WT("p7s_cols", s_v)
                I(eng.memset(amaxA[:, 0:1024], 0.0), s_v)
                M("p8_amz", s_v)
            for oc in range(NHC):
                tag = f"p8o{oc}"
                gin = ginb(oc)
                if E == "sync":
                    WT(f"p7w1o{oc}_sp3", s_sp)
                    if oc == 0:
                        WT(f"p7w1o{NHC - 1}_sp3", s_sp)
                    if oc >= 2:
                        WT(f"p8o{oc - 2}_g2", s_v)
                    I(eng.dma_start(out=gin[:, 0:2048],
                                    in_=gate_h.ap()[oc * 128:(oc + 1) * 128, :]),
                      s_ld, 16)
                    M(tag + "_ld", s_ld)
                for h in (0, 1):
                    sub = slice(h * 1024, (h + 1) * 1024)
                    if E == "scalar":
                        if h == 0:
                            WT(tag + "_ld", s_ld)
                            WT("p7s_cols", s_v)
                        if oc > 0:
                            WT(f"p8o{oc - 1}_d{h}", s_v)
                        I(eng.activation(tf[:, sub], gin[:, sub], AF.Identity,
                                         bias=cmag[:, 0:1], scale=scol[:, 5:6]),
                          s_s)
                        M(f"{tag}_a{h}", s_s)
                    if E == "vector":
                        WT(f"{tag}_a{h}", s_s)
                        I(eng.tensor_scalar(gin[:, sub], tf[:, sub], MAGIC,
                                            scol[:, 4:5], ALU.subtract,
                                            ALU.mult), s_v)
                        M(f"{tag}_b{h}", s_v)
                    if E == "scalar":
                        WT(f"{tag}_b{h}", s_v)
                        I(eng.activation(tf[:, sub], gin[:, sub], AF.Sigmoid,
                                         bias=0.0, scale=1.0), s_s)
                        M(f"{tag}_c{h}", s_s)
                    if E == "vector":
                        WT(f"{tag}_c{h}", s_s)
                        if h == 0 and oc > 0:
                            WT(f"p8o{oc - 1}_sp", s_sp)
                        I(eng.tensor_scalar(tf[:, sub], tf[:, sub], 192.0,
                                            192.0, ALU.add, ALU.subtract), s_v)
                        I(eng.scalar_tensor_tensor(g2o[:, sub], tf[:, sub],
                                                   U16C, gin[:, sub],
                                                   ALU.min, ALU.mult), s_v)
                        am8 = oc * 2 + h
                        I(eng.tensor_reduce(
                            amaxA[:, am8:am8 + 1].unsqueeze(2),
                            g2o[:, sub].rearrange("p (a x) -> p a x", a=1),
                            AX.X, ALU.max, apply_absolute_value=True), s_v)
                        M(f"{tag}_d{h}", s_v)
                        if h == 1:
                            M(tag + "_g2", s_v)
                if E == "gpsimd":
                    WT(tag + "_g2", s_v)
                    I(eng.dma_start(out=g2_h.ap()[oc * 128:(oc + 1) * 128, :],
                                    in_=g2o[:, 0:2048]), s_sp, 16)
                    M(tag + "_sp", s_sp)
            emit_scale_ag(E, eng, I, M, WT, WSelf, IW, 2, "p8s", amaxA, 0, [("sg2", 6)])

        def emit_p9(E, eng, I, M, WT, WSelf, IW):
            upres = vf16(AR_BIG, 0, 49152)
            gin = vf16(AR_SM, 0, 2048)
            tf = vf32(AR_SM, 4096, 2048)
            r1f = AR_FX[:, 0:2048]
            if E == "vector":
                WT("p8s_cols", s_v)
                I(eng.memset(amaxA[:, 0:1024], 0.0), s_v)
                M("p9_amz", s_v)
            for oc in range(NHC):
                tag = f"p9o{oc}"
                if E == "sync":
                    WT(f"p8o{oc}_sp", s_sp)
                    if oc > 0:
                        WT(f"p9o{oc - 1}_a1b", s_s)
                    else:
                        WT(f"p8o{NHC - 1}_g2", s_v)
                    I(eng.dma_start(out=gin[:, 0:2048],
                                    in_=g2_h.ap()[oc * 128:(oc + 1) * 128, :]),
                      s_ld, 16)
                    M(tag + "_ld", s_ld)
                for h in (0, 1):
                    sub = slice(h * 1024, (h + 1) * 1024)
                    usub = slice(oc * 2048 + h * 1024, oc * 2048 + (h + 1) * 1024)
                    if E == "scalar":
                        if h == 0:
                            WT(tag + "_ld", s_ld)
                            WT("p8s_cols", s_v)
                            if oc == 0:
                                WT(f"p8o{NHC - 1}_sp", s_sp)
                        if oc > 0:
                            WT(f"p9o{oc - 1}_w{h}", s_v)
                            if oc == 1 and h == 0 and KDUMP == "oi":
                                WT("park_oi", s_sp)
                        I(eng.activation(tf[:, sub], gin[:, sub], AF.Identity,
                                         bias=cmag[:, 0:1], scale=scol[:, 7:8]),
                          s_s)
                        M(f"{tag}_a1{h}", s_s)
                        if h == 1:
                            M(tag + "_a1b", s_s)
                    if E == "vector":
                        WT(f"{tag}_a1{h}", s_s)
                        I(eng.tensor_scalar(r1f[:, sub], tf[:, sub], MAGIC,
                                            None, ALU.subtract), s_v)
                        M(f"{tag}_v1{h}", s_v)
                    if E == "scalar":
                        WT(f"{tag}_v1{h}", s_v)
                        I(eng.activation(tf[:, sub], upres[:, usub],
                                         AF.Identity, bias=cmag[:, 0:1],
                                         scale=scol[:, 3:4]), s_s)
                        M(f"{tag}_a2{h}", s_s)
                    if E == "vector":
                        WT(f"{tag}_a2{h}", s_s)
                        I(eng.scalar_tensor_tensor(
                            vbf(AR_BIG, oc * 2048 + h * 1024, 1024)[:, 0:1024],
                            tf[:, sub], cmag[:, 1:2], r1f[:, sub],
                            ALU.add, ALU.mult), s_v)
                        am9 = oc * 2 + h
                        I(eng.tensor_reduce(
                            amaxA[:, am9:am9 + 1].unsqueeze(2),
                            vbf(AR_BIG, oc * 2048 + h * 1024, 1024)[:, 0:1024]
                            .rearrange("p (a x) -> p a x", a=1),
                            AX.X, ALU.max, apply_absolute_value=True), s_v)
                        M(f"{tag}_w{h}", s_v)
                        if h == 1:
                            M(tag + "_oi", s_v)
            if E == "gpsimd" and KDUMP == "oi":
                WT("p9o0_oi", s_v)
                I(eng.dma_start(out=park.ap()[:, 35968:38016],
                                in_=AR_BIG[:].bitcast(mybir.dt.uint16)
                                [:, 0:2048]), s_sp, 16)
                I(eng.dma_start(out=park.ap()[:, 38016:40064],
                                in_=AR_SM[:].bitcast(mybir.dt.uint16)
                                [:, 2048:4096]), s_sp, 16)
                M("park_oi", s_sp)
            emit_scale_ag(E, eng, I, M, WT, WSelf, IW, 3, "p9s", amaxA, 0, [("so", 8)])
            if E == "vector":
                WT("p9s_cols", s_v)
                IW(eng.tensor_scalar(scol[:, 10:11], scol[:, 9:10], 0.0, None,
                                     ALU.add), s_v)
                IW(eng.tensor_tensor(scol[:, 8:9], scol[:, 8:9], scol[:, 6:7],
                                     ALU.mult), s_v)
                IW(eng.tensor_tensor(scol[:, 8:9], scol[:, 8:9], scol[:, 2:3],
                                     ALU.mult), s_v)
                M("p9_kap", s_v)

        def emit_p10(E, eng, I, M, WT, WSelf, IW):
            tf = vf32(AR_SM, 4096, 2048)
            if E == "gpsimd" and KDUMP == "oq":
                WT("p10_oq", s_v)
                I(eng.dma_start(out=park.ap()[:, 35968:38016],
                                in_=AR_BIG[:].bitcast(mybir.dt.uint16)
                                [:, 0:2048]), s_sp, 16)
                M("park_oq", s_sp)
            for oc in range(NHC):
                tag = f"p10q{oc}"
                for h in (0, 1):
                    qsub = slice(h * 1024, (h + 1) * 1024)
                    if E == "scalar":
                        if oc == 0 and h == 0:
                            WT("p9_kap", s_v)
                        if oc > 0:
                            WT(f"p10q{oc - 1}_q{h}", s_v)
                        I(eng.activation(tf[:, qsub],
                                         vbf(AR_BIG, oc * 2048 + h * 1024,
                                             1024)[:, 0:1024],
                                         AF.Identity, bias=cmag[:, 0:1],
                                         scale=scol[:, 10:11]), s_s)
                        M(f"{tag}_t{h}", s_s)
                        if h == 1:
                            M(tag + "_t", s_s)
                    if E == "vector":
                        WT(f"{tag}_t{h}", s_s)
                        I(eng.tensor_scalar(vbf(AR_BIG, oc * 2048 + h * 1024,
                                                1024)[:, 0:1024],
                                            tf[:, qsub], MAGIC, scol[:, 8:9],
                                            ALU.subtract, ALU.mult), s_v)
                        M(f"{tag}_q{h}", s_v)
                        if h == 1:
                            M(tag + "_q", s_v)
            if E == "vector":
                M("p10_oq", s_v)
            for oc in range(8):
                for hcb in range(3):
                    tag = f"p10o{oc}b{hcb}"
                    flat = oc * 3 + hcb
                    half = (flat % 2) * 1024
                    if E == "sync":
                        if flat == 0:
                            WT(f"p7w1o{NHC - 1}_mm", s_t)
                        elif flat >= 2:
                            pp = flat - 2
                            WT(f"p10o{pp // 3}b{pp % 3}_mm", s_t)
                        ti0 = wgud_tidx(2, hcb * 8, oc)
                        I(eng.dma_start(
                            out=wtile[:, half:half + 1024]
                            .rearrange("p (t q) -> p t q", q=128),
                            in_=ag2o.rearrange("(t p) q -> t p q", p=128)
                            [ti0:ti0 + 8, :, :].transpose([1, 0, 2])),
                          s_ld, 16)
                        M(tag + "_w", s_ld)
                    if E == "tensor":
                        WT(tag + "_w", s_ld)
                        if oc == 0:
                            WT(f"p10q{hcb * 8 + 7}_q", s_v)
                        if hcb == 0 and oc > 0:
                            WT(f"p10o{oc - 1}_add", s_v)
                        def pbank10(o, t):
                            return PS_A[:, t * 512:(t + 1) * 512]
                        for hcl in range(8):
                            hc = hcb * 8 + hcl
                            for tg in range(4):
                                I(eng.matmul(
                                    pbank10(oc, tg),
                                    wtile[:, half + hcl * 128:
                                          half + (hcl + 1) * 128],
                                    vbf(AR_BIG, hc * 2048 + G4[tg], 512)[:, 0:512],
                                    start=(hc == 0), stop=(hc == NHC - 1)), s_t)
                        M(tag + "_mm", s_t)
                for tg in range(4):
                    g = G4[tg]
                    tag = f"p10o{oc}t{tg}"
                    if E == "sync":
                        WT("p5_x2", s_sp)
                        if oc > 0 or tg > 0:
                            WT(f"p10o{oc}t{tg - 1}_add" if tg > 0 else
                               f"p10o{oc - 1}t3_add", s_v)
                        I(eng.dma_start(out=AR_FX[:, 0:512],
                                        in_=x2_h.ap()[oc * 128:(oc + 1) * 128,
                                                      g:g + 512]), s_ld, 16)
                        M(tag + "_x2", s_ld)
                    if E == "vector":
                        WT(tag + "_x2", s_ld)
                        WT(f"p10o{oc}b2_mm", s_t)
                        if oc > 0 or tg > 0:
                            WT(f"p10o{oc}t{tg - 1}_st" if tg > 0 else
                               f"p10o{oc - 1}t3_st", s_sp)
                        I(eng.tensor_tensor(AR_FX[:, 1024:1536],
                                            PS_A[:, tg * 512:(tg + 1) * 512],
                                            AR_FX[:, 0:512], ALU.add), s_v)
                        M(tag + "_add", s_v)
                        if tg == 3:
                            M(f"p10o{oc}_add", s_v)
                    if E == "gpsimd":
                        WT(tag + "_add", s_v)
                        I(eng.dma_start(out=out_h[oc * 128:(oc + 1) * 128, g:g + 512],
                                        in_=AR_FX[:, 1024:1536]), s_sp, 16)
                        M(tag + "_st", s_sp)
            if E == "gpsimd":
                for i in range(4):
                    I(eng.dma_start(out=dbg_ag.ap()[i * 8:(i + 1) * 8, :],
                                    in_=agO_out[i].ap()), s_sp, 16)
                I(eng.dma_start(out=dbg_ag.ap()[32:40, :],
                                in_=ag0_out.ap()[:, 0:8]), s_sp, 16)
                WSelf(s_sp)
                ou = out_h.bitcast(mybir.dt.uint16)
                if KDUMP == "x2":
                    I(eng.dma_start(out=out_h, in_=x2_h.ap()), s_sp, 16)
                elif KDUMP == "h2":
                    I(eng.dma_start(out=ou[:, 0:TOK],
                                    in_=h2_h.ap().bitcast(mybir.dt.uint16)), s_sp, 16)
                elif KDUMP in ("gate", "g2"):
                    srcp = gate_h if KDUMP == "gate" else g2_h
                    I(eng.dma_start(out=ou[:, 0:2 * TOK],
                                    in_=srcp.ap().bitcast(mybir.dt.uint16)
                                    .rearrange("(a b) t -> a (b t)", b=2)
                                    [0:D, 0:2 * TOK]), s_sp, 16)
                elif KDUMP == "gate2" or KDUMP == "g22":
                    srcp = gate_h if KDUMP == "gate2" else g2_h
                    I(eng.dma_start(out=ou[:, 0:2 * TOK],
                                    in_=srcp.ap().bitcast(mybir.dt.uint16)
                                    .rearrange("(a b) t -> a (b t)", b=2)
                                    [D:D + 512, 0:2 * TOK]), s_sp, 16)
                elif KDUMP == "ag":
                    I(eng.dma_start(out=out_h[0:4, 0:8],
                                    in_=dbg_ag.ap()[0:4 * 8, :]
                                    .rearrange("(a b) c -> a (b c)", b=8)[0:4, 0:8]
                                    if False else dbg_ag.ap()[0:4, 0:8]), s_sp, 16)
                    I(eng.dma_start(out=out_h[4:44, 0:8],
                                    in_=dbg_ag.ap()), s_sp, 16)
                elif KDUMP == "qkb":
                    I(eng.dma_start(out=ou[0:128, 0:4096],
                                    in_=park.ap()[:, 0:4096]), s_sp, 16)
                elif KDUMP.startswith("qrot"):
                    vi = int(KDUMP[4:])
                    I(eng.dma_start(out=ou[0:128, 0:4096],
                                    in_=park.ap()[:, 9216 + vi * 4096:
                                                  9216 + (vi + 1) * 4096]), s_sp, 16)
                elif KDUMP == "krot":
                    I(eng.dma_start(out=ou[0:128, 0:4096],
                                    in_=park.ap()[:, 9216 + 12288:
                                                  9216 + 12288 + 4096]), s_sp, 16)
                elif KDUMP == "sim":
                    I(eng.dma_start(out=ou[0:128, 0:768],
                                    in_=park.ap()[:, 26624:27392]), s_sp, 16)
                elif KDUMP == "attno":
                    I(eng.dma_start(out=ou[0:128, 0:4096],
                                    in_=park.ap()[:, 27392:31488]), s_sp, 16)
                    I(eng.dma_start(out=ou[128:256, 0:4096],
                                    in_=park.ap()[:, 31488:35584]), s_sp, 16)
                elif KDUMP == "x2e":
                    I(eng.dma_start(out=ou[0:128, 0:1024],
                                    in_=park.ap()[:, 35744:36768]), s_sp, 16)
                    I(eng.dma_start(out=ou[128:256, 0:1024],
                                    in_=park.ap()[:, 36768:37792]), s_sp, 16)
                    I(eng.dma_start(out=ou[256:384, 0:1024],
                                    in_=park.ap()[:, 37792:38816]), s_sp, 16)
                elif KDUMP == "sc":
                    I(eng.dma_start(out=ou[0:128, 0:160],
                                    in_=park.ap()[:, 35584:35744]), s_sp, 16)
                elif KDUMP == "wq3":
                    I(eng.dma_start(out=ou[0:128, 0:128],
                                    in_=park.ap()[:, 35584:35712]), s_sp, 16)
                elif KDUMP == "wq4":
                    I(eng.dma_start(out=ou[0:128, 0:512],
                                    in_=park.ap()[:, 35968:36480]), s_sp, 16)
                elif KDUMP == "wq5":
                    I(eng.dma_start(out=ou[0:128, 0:512],
                                    in_=park.ap()[:, 36480:36992]), s_sp, 16)
                elif KDUMP == "wq6":
                    I(eng.dma_start(out=ou[0:128, 0:544],
                                    in_=park.ap()[:, 37120:37664]), s_sp, 16)
                elif KDUMP == "oq":
                    I(eng.dma_start(out=ou[0:128, 0:2048],
                                    in_=park.ap()[:, 35968:38016]), s_sp, 16)
                elif KDUMP == "oi":
                    I(eng.dma_start(out=ou[0:128, 0:2048],
                                    in_=park.ap()[:, 35968:38016]), s_sp, 16)
                    I(eng.dma_start(out=ou[128:256, 0:2048],
                                    in_=park.ap()[:, 38016:40064]), s_sp, 16)
                elif KDUMP == "wq2":
                    for j in range(8):
                        I(eng.dma_start(out=ou[j * 128:(j + 1) * 128, 0:128],
                                        in_=ag2_in.ap()[j * 128:(j + 1) * 128, :]),
                          s_sp, 16)
                elif KDUMP == "wq":
                    I(eng.dma_start(out=ou[0:128, 0:384],
                                    in_=park.ap()[:, 35584:35968]), s_sp, 16)
                elif KDUMP == "wg":
                    for j in range(8):
                        tj = wgud_tidx(KHSEL, j, KWSEL)
                        I(eng.dma_start(out=ou[j * 128:(j + 1) * 128, 0:128],
                                        in_=ag2o[tj * 128:(tj + 1) * 128, :]),
                          s_sp, 16)
                elif KDUMP == "wi":
                    I(eng.dma_start(out=ou[0:512, 0:128],
                                    in_=ag1_in.ap()[1024:1536, :]), s_sp, 16)
                elif KDUMP == "wo":
                    for j in range(4):
                        I(eng.dma_start(out=ou[j * 128:(j + 1) * 128, 0:128],
                                        in_=ag1o[(KWSEL * 12 + 8 + j) * 128:
                                                 (KWSEL * 12 + 9 + j) * 128, :]),
                          s_sp, 16)
                elif KDUMP == "ms":
                    I(eng.dma_start(out=out_h[0:128, 0:128], in_=Mall[:]), s_sp, 16)
                    I(eng.dma_start(out=out_h[0:128, 128:256], in_=Sall[:]), s_sp, 16)
                    I(eng.dma_start(out=out_h[0:128, 256:384], in_=biasM[:]), s_sp, 16)
                elif KDUMP == "nb":
                    I(eng.dma_start(out=ou[0:128, 0:128],
                                    in_=nbias_bnc.ap().bitcast(mybir.dt.uint16)),
                      s_sp, 16)
                elif KDUMP == "vh":
                    I(eng.dma_start(out=ou[:, 0:512],
                                    in_=v_h.ap().bitcast(mybir.dt.uint16)
                                    [0:D, 0:512]), s_sp, 16)
                if KDUMP:
                    WSelf(s_sp)

        @block.sync
        def _(sync):
            emit("sync", sync)

        @block.vector
        def _(vector):
            emit("vector", vector)

        @block.scalar
        def _(scalar):
            emit("scalar", scalar)

        @block.tensor
        def _(tensor):
            emit("tensor", tensor)

        @block.gpsimd
        def _(gpsimd):
            emit("gpsimd", gpsimd)

    return nc, counts


def _shard_inputs(inputs):
    x = np.asarray(inputs["L0_attn_x"])
    t = _tables()
    per_core = []
    xf = np.ascontiguousarray(x.reshape(SEQ, D))
    zpad = np.zeros((HALO, D), np.float32)
    lng = np.ascontiguousarray(np.asarray(inputs["L0_attn_ln_g"]).reshape(8, 128).T)
    lnb = np.ascontiguousarray(np.asarray(inputs["L0_attn_ln_b"]).reshape(8, 128).T)
    cc = np.ascontiguousarray(np.asarray(inputs["L0_c"]).reshape(8, 128).T)
    wqkv = np.asarray(inputs["L0_attn_w_qkv"])
    wout = np.asarray(inputs["L0_attn_w_out"])
    wg = np.asarray(inputs["L0_ffn_w_gate"])
    wu = np.asarray(inputs["L0_ffn_w_up"])
    wd = np.asarray(inputs["L0_ffn_w_down"])
    wmod = np.asarray(inputs["L0_wmod"])
    for c in range(NC):
        lo, hi = c * TOK - HALO, (c + 1) * TOK + HALO
        parts = []
        if lo < 0:
            parts.append(zpad)
            parts.append(xf[0:hi])
        elif hi > SEQ:
            parts.append(xf[lo:SEQ])
            parts.append(zpad)
        else:
            parts.append(xf[lo:hi])
        xp = np.ascontiguousarray(np.concatenate(parts, 0).T)
        sel = np.zeros((128, 8), np.float32)
        sel[:, c] = 1.0
        m = dict(
            x_pad=xp, lng_c=lng, lnb_c=lnb, c_cols=cc,
            wmod_sl=np.ascontiguousarray(wmod[:, c * 768:(c + 1) * 768]),
            wqkv_sl=np.ascontiguousarray(wqkv[c * 128:(c + 1) * 128, :]),
            wout_sl=np.ascontiguousarray(wout[c * 128:(c + 1) * 128, :]),
            wg_sl=np.ascontiguousarray(wg[c * 384:(c + 1) * 384, :]),
            wu_sl=np.ascontiguousarray(wu[c * 384:(c + 1) * 384, :]),
            wd_sl=np.ascontiguousarray(wd[c * 128:(c + 1) * 128, :]),
            sel=sel,
        )
        for n in TBF + ["ident"]:
            v = t[n]
            if n == "mneg_w0" and c != 0:
                v = t["mneg_mid"]
            if n == "mneg_w15" and c != NC - 1:
                v = t["mneg_mid"]
            if n == "m01_lw0" and c == 0:
                v = t["m01_z"]
            if n == "m01_rw15" and c == NC - 1:
                v = t["m01_z"]
            m["t_" + n] = v
        per_core.append(m)
    return per_core


def kernel_debug(**inputs):
    _, counts = build_program(None)
    nc, _ = build_program(counts)
    in_maps = _shard_inputs(inputs)
    res = run_bass_kernel_spmd(nc, in_maps, core_ids=list(range(NC)))
    return res.results


def kernel(**inputs):
    if "prog" not in _CACHE:
        _, counts = build_program(None)
        nc, _ = build_program(counts)
        _CACHE["prog"] = nc
    nc = _CACHE["prog"]
    in_maps = _shard_inputs(inputs)
    res = run_bass_kernel_spmd(nc, in_maps, core_ids=list(range(NC)))
    out = np.concatenate([np.ascontiguousarray(np.asarray(r["out"]).T)
                          for r in res.results], 0)
    return out.reshape(1, SEQ, D)
